# version 18
# baseline (speedup 1.0000x reference)
import contextlib
import math

import numpy as np
import ml_dtypes

import concourse.bass as bass
import concourse.mybir as mybir
from concourse.bass_utils import run_bass_kernel_spmd

F32 = mybir.dt.float32
BF16 = mybir.dt.bfloat16
AF = mybir.ActivationFunctionType
ALU = mybir.AluOpType
AX = mybir.AxisListType.X

NEGM = -1.0e30
EPS = 1e-5
ENGINES = ("sync", "tensor", "scalar", "vector", "gpsimd")


class CFG:
    def __init__(self, D=2048, S=2048, NE=16, TOPK=6, I1=1408, IS=2816):
        self.D, self.S, self.NE, self.TOPK, self.I1, self.IS = D, S, NE, TOPK, I1, IS
        assert D % 128 == 0 and S % 1024 == 0 and I1 % 128 == 0 and IS % 256 == 0
        self.HC = D // 128
        self.NH = self.HC
        self.NT = S // 128
        self.HALF = S // 2
        self.NTH = self.HALF // 128
        self.J2 = I1 // 128
        self.JS2 = IS // 128
        assert IS == 2 * I1, "shared expert must split into two expert-shaped subs"
        assert self.JS2 % 2 == 0
        self.ALPHA = 1.0 / math.sqrt(128.0)


class Emit:

    def __init__(self, nc, stack):
        self.nc = nc
        self.stack = stack
        self.qs = {e: [] for e in ENGINES}
        self.sems = {}
        self.cnt = {}
        self.lastwait = {e: {} for e in ENGINES}
        for s in ("dW", "PE", "SC", "VE", "PO"):
            self.sem(s)

    def sem(self, s):
        if s not in self.sems:
            self.sems[s] = self.stack.enter_context(self.nc.semaphore(s))
            self.cnt[s] = 0
        return self.sems[s]

    strict = False
    strict_phases = None
    phase = "pre"

    def fence(self, eng):
        sem = {"vector": "VE", "scalar": "SC", "gpsimd": "PO", "tensor": "PE"}[eng]
        return self.op(eng, lambda e: e.drain(fusable=True), inc=sem, pub=False)

    def op(self, eng, fn, waits=(), inc=None, incv=1, pub=None):
        s_on = (Emit.strict if Emit.strict_phases is None
                else (Emit.phase in Emit.strict_phases))
        if s_on:
            waits = list(waits) + [(s, v) for s, v in self.cnt.items()]
        lw = self.lastwait[eng]
        eff = []
        for s, v in waits:
            if v and v > lw.get(s, 0):
                self.sem(s)
                eff.append((s, v))
                lw[s] = v
        after = None
        if inc is not None:
            self.sem(inc)
            self.cnt[inc] += incv
            after = self.cnt[inc]
        sems = self.sems
        if pub is None:
            pub = inc in ("SC", "VE", "PO")

        def closure(e, fn=fn, eff=tuple(eff), inc=inc, incv=incv, pub=pub):
            for s, v in eff:
                e.wait_ge(sems[s], v)
            instr = fn(e)
            if inc is not None:
                assert instr is not None, "op with inc must return an instruction"
                if pub:
                    e.drain(fusable=True).then_inc(sems[inc], incv)
                else:
                    instr.then_inc(sems[inc], incv)

        self.qs[eng].append(closure)
        return after

    def now(self, s):
        return self.cnt[s]

    def flush(self, block):
        for e in ENGINES:
            fns = self.qs[e]

            def runner(engine, fns=fns):
                for f in fns:
                    f(engine)

            getattr(block, e)(runner)


class WStream:

    def __init__(self, E, stack, nc, ns, cols, name):
        self.E, self.ns, self.cols = E, ns, cols
        self.wsf = [stack.enter_context(nc.sbuf_tensor(f"{name}f{i}", [128, cols], F32))
                    for i in range(ns)]
        self.wsb = [stack.enter_context(nc.sbuf_tensor(f"{name}b{i}", [128, cols], BF16))
                    for i in range(ns)]
        self.i = 0
        self.cast_done = {}
        self.use_done = {}
        self.sem_of = {"gpsimd": "PO", "scalar": "SC", "vector": "VE"}
        self.rr = ("gpsimd", "vector", "scalar")

    def load(self, dram_ap, ncols, cast=None):
        E = self.E
        i = self.i
        if cast is None:
            cast = "gpsimd"
        self.i += 1
        slot = i % self.ns
        prev = i - self.ns
        waits = [self.cast_done[prev]] if prev >= 0 else []
        dsem = f"wsS{slot}"
        dv = E.op("sync",
                  lambda e, d=dram_ap, s=slot, w=ncols: e.dma_start(self.wsf[s][:, 0:w], d),
                  waits=waits, inc=dsem, incv=16)
        cwaits = [(dsem, dv)]
        if prev >= 0:
            assert prev in self.use_done, f"slab {prev} consumers not recorded"
            cwaits.append(("PE", self.use_done[prev]))
        sem = self.sem_of[cast]
        if cast == "scalar":
            fn = lambda e, s=slot, w=ncols: e.copy(self.wsb[s][:, 0:w],
                                                   self.wsf[s][:, 0:w])
        else:
            fn = lambda e, s=slot, w=ncols: e.tensor_copy(self.wsb[s][:, 0:w],
                                                          self.wsf[s][:, 0:w])
        cv = E.op(cast, fn, waits=cwaits, inc=sem)
        self.cast_done[i] = (sem, cv)
        return slot, i

    def ready(self, idx):
        return self.cast_done[idx]

    def done(self, idx, pe_val):
        self.use_done[idx] = pe_val


def build_nc(cfg=None, dbg=()):
    c = cfg or CFG()
    HC, NT, NE, AL = c.HC, c.NT, c.NE, c.ALPHA
    NH, S, D, HALF = c.NH, c.S, c.D, c.HALF
    J2, JS2 = c.J2, c.JS2
    NTC4 = S // 512
    nc = bass.Bass("TRN2", target_bir_lowering=False, detect_race_conditions=False)

    dt = nc.dram_tensor
    xT = dt("xT", [D, S], F32, kind="ExternalInput").ap()
    aw = dt("aw", [3 * NH, 128, HC, 128], F32, kind="ExternalInput").ap()
    ab = dt("ab", [128, 3 * NH], F32, kind="ExternalInput").ap()
    pw = dt("pw", [HC, 128, HC, 128], F32, kind="ExternalInput").ap()
    pb = dt("pb", [128, HC], F32, kind="ExternalInput").ap()
    gw = dt("gw", [HC, 128, NE], F32, kind="ExternalInput").ap()
    w13 = dt("w13", [NE, 2 * J2, 128, HC, 128], F32, kind="ExternalInput").ap()
    w2 = dt("w2", [NE, HC, 128, J2, 128], F32, kind="ExternalInput").ap()
    sw13 = dt("sw13", [2 * JS2, 128, HC, 128], F32, kind="ExternalInput").ap()
    sw2 = dt("sw2", [HC, 128, JS2, 128], F32, kind="ExternalInput").ap()
    cst = dt("cst", [128, 384], F32, kind="ExternalInput").ap()
    outT = dt("outT", [D, S], F32, kind="ExternalOutput").ap()
    x1T = dt("x1T", [D, S], F32, kind="Internal").ap()
    aTs = dt("aTs", [D, S], F32, kind="Internal").ap()
    ln2s = dt("ln2s", [D, S], BF16, kind="Internal").ap()
    combTd = dt("combTd", [NE, S], F32, kind="Internal").ap()
    dbg_t = {}
    for name, shape, dtp in [
        ("dbg_ln1", [D, S], BF16), ("dbg_attn", [128, NH * S], BF16),
        ("dbg_x1", [D, S], F32), ("dbg_ln2", [D, S], BF16),
        ("dbg_comb", [NE, S], F32), ("dbg_acc", [D, HALF], F32),
        ("dbg_a", [D, S], F32), ("dbg_acc1", [D, HALF], F32),
        ("dbg_lnr", [128, 16384], BF16),
    ]:
        if name in dbg:
            dbg_t[name] = dt(name, shape, dtp, kind="ExternalOutput").ap()

    with contextlib.ExitStack() as st:
        en = st.enter_context
        E = Emit(nc, st)
        op, now = E.op, E.now

        ident = en(nc.sbuf_tensor("ident", [128, 128], F32))
        cmask = en(nc.sbuf_tensor("cmaskS", [128, 128], F32))
        cmaskT = en(nc.sbuf_tensor("cmaskTS", [128, 128], F32))
        ones1b = en(nc.sbuf_tensor("ones1b", [1, 128], BF16))
        ones1f = en(nc.sbuf_tensor("ones1f", [1, 128], F32))
        onescb = en(nc.sbuf_tensor("onescb", [128, 1], BF16))
        onescf = en(nc.sbuf_tensor("onescf", [128, 1], F32))
        epst = en(nc.sbuf_tensor("epst", [128, 1], F32))
        abs_ = en(nc.sbuf_tensor("abs_", [128, 3 * NH], F32))
        pbs = en(nc.sbuf_tensor("pbs", [128, HC], F32))
        gws = en(nc.sbuf_tensor("gws", [128, HC * NE], F32))
        gsum = en(nc.sbuf_tensor("gsum", [1, NE], F32))
        lnA = en(nc.sbuf_tensor("lnA", [128, HC * HALF], BF16))
        sstat = en(nc.sbuf_tensor("sstat", [128, 4 * 512], BF16))
        mrow = en(nc.sbuf_tensor("mrow", [1, 512], F32))
        vrow = en(nc.sbuf_tensor("vrow", [1, 512], F32))
        trow = en(nc.sbuf_tensor("trow", [1, 512], F32))
        rrow = en(nc.sbuf_tensor("rrow", [1, 512], F32))
        fsc = en(nc.sbuf_tensor("fsc", [128, 2 * 512], F32))
        SLABC = max(HC * 128, J2 * 128)
        ws = WStream(E, st, nc, 3, SLABC, "ws")
        PS = en(nc.psum_tensor("PS", [128, 4096], F32))

        def bk(i, lo=0, hi=512):
            return PS[:, i * 512 + lo:i * 512 + hi]

        def bkr(i, lo, hi, p0=0, p1=1):
            return PS[p0:p1, i * 512 + lo:i * 512 + hi]

        for dst, src in [(ident[:], cst[:, 0:128]), (cmask[:], cst[:, 128:256]),
                         (cmaskT[:], cst[:, 256:384]), (abs_[:], ab), (pbs[:], pb)]:
            op("sync", lambda e, d=dst, s=src: e.dma_start(d, s), inc="dW", incv=16)
        d_gw = op("sync", lambda e: e.dma_start(
            gws[:].rearrange("p (h e) -> p h e", h=HC),
            gw.rearrange("h p e -> p h e")), inc="dW", incv=16)
        op("vector", lambda e: e.memset(ones1b[:], 1.0), inc="VE", pub=False)
        op("vector", lambda e: e.memset(ones1f[:], 1.0), inc="VE", pub=False)
        op("vector", lambda e: e.memset(onescb[:], 1.0), inc="VE", pub=False)
        op("vector", lambda e: e.memset(onescf[:], 1.0), inc="VE")
        ve_pre = op("vector", lambda e: e.memset(epst[:], EPS), inc="VE")

        for h in range(HC):
            op("tensor", lambda e, h=h: e.matmul(
                bkr(2, 0, NE), onescf[:], gws[:, h * NE:(h + 1) * NE],
                start=(h == 0), stop=(h == HC - 1)),
                waits=[("dW", d_gw), ("VE", ve_pre)],
                inc="PE" if h == HC - 1 else None)
        sc_gsum = op("scalar", lambda e: e.copy(gsum[:], bkr(2, 0, NE)),
                     waits=[("PE", now("PE"))], inc="SC")

        def emit_ln_stats(src_fn, tw, bsum, bsq, bm, br, waits):
            pe_val = {}
            for h in range(HC):
                r = h % 2
                w = list(waits)
                if h >= 2:
                    w = w + [("PE", pe_val[h - 2])]
                src_ap = src_fn(h)
                v1 = op("vector", lambda e, a=src_ap, r=r: e.tensor_copy(
                    sstat[:, r * 1024:r * 1024 + tw], a), waits=w, inc="VE")
                s1 = op("scalar", lambda e, a=src_ap, r=r: e.activation(
                    sstat[:, r * 1024 + 512:r * 1024 + 512 + tw], a, AF.Square),
                    waits=w, inc="SC")
                op("tensor", lambda e, h=h, r=r: e.matmul(
                    bkr(bsum, 0, tw), onescb[:], sstat[:, r * 1024:r * 1024 + tw],
                    start=(h == 0), stop=(h == HC - 1)),
                    waits=[("VE", v1)], inc="PE")
                op("tensor", lambda e, h=h, r=r: e.matmul(
                    bkr(bsq, 0, tw), onescb[:],
                    sstat[:, r * 1024 + 512:r * 1024 + 512 + tw],
                    start=(h == 0), stop=(h == HC - 1)),
                    waits=[("SC", s1)], inc="PE")
                pe_val[h] = now("PE")
            pe_sums = now("PE")
            op("vector", lambda e: e.tensor_scalar(
                mrow[:, 0:tw], bkr(bsum, 0, tw), 1.0 / D, None, ALU.mult),
                waits=[("PE", pe_sums)], inc="VE")
            op("vector", lambda e: e.tensor_scalar(
                vrow[:, 0:tw], bkr(bsq, 0, tw), 1.0 / D, None, ALU.mult),
                inc="VE")
            op("vector", lambda e: e.tensor_mul(trow[:, 0:tw], mrow[:, 0:tw],
                                                mrow[:, 0:tw]), inc="VE", pub=False)
            ve_m = op("vector", lambda e: e.tensor_sub(vrow[:, 0:tw], vrow[:, 0:tw],
                                                       trow[:, 0:tw]), inc="VE")
            op("scalar", lambda e: e.activation(rrow[:, 0:tw], vrow[:, 0:tw], AF.Ln,
                                                bias=epst[0:1, :]),
               waits=[("VE", ve_m)], inc="SC")
            sc_r = op("scalar", lambda e: e.activation(rrow[:, 0:tw], rrow[:, 0:tw],
                                                       AF.Exp, scale=-0.5), inc="SC")
            op("tensor", lambda e: e.matmul(bk(bm, 0, tw), ones1f[:], mrow[:, 0:tw],
                                            start=True, stop=True),
               waits=[("VE", ve_m)], inc="PE")
            pe_bc = op("tensor", lambda e: e.matmul(bk(br, 0, tw), ones1f[:],
                                                    rrow[:, 0:tw], start=True, stop=True),
                       waits=[("SC", sc_r)], inc="PE")
            return {"pe_bcast": pe_bc, "sc_r": sc_r, "ve_m": ve_m, "pe_sums": pe_sums}

        Emit.phase = "p0"
        ln1_stack = contextlib.ExitStack()
        ln1b = ln1_stack.enter_context(
            nc.sbuf_tensor("ln1b", [128, HC * HALF], BF16, side="right"))

        def ln1_ap(hcx, t0, twid):
            if t0 < HALF:
                assert t0 + twid <= HALF
                return lnA[:, hcx * HALF + t0:hcx * HALF + t0 + twid]
            return ln1b[:, hcx * HALF + (t0 - HALF):hcx * HALF + (t0 - HALF) + twid]

        with nc.sbuf_tensor("xc0", [128, HC * 512], F32) as xc0:
            last_ve = 0
            for tc in range(NTC4):
                wd = [("VE", ve_pre)]
                if tc > 0:
                    wd += [("VE", now("VE")), ("SC", now("SC"))]
                dv = op("gpsimd", lambda e, tc=tc: e.dma_start(
                    xc0[:].rearrange("p (h t) -> p h t", h=HC),
                    xT[:, tc * 512:(tc + 1) * 512].rearrange("(h p) t -> p h t", p=128)),
                    waits=wd, inc="sXC0", incv=16)
                d0 = emit_ln_stats(lambda h: xc0[:, h * 512:(h + 1) * 512], 512,
                                   2, 3, 4, 5, [("sXC0", dv), ("SC", sc_gsum)])
                for h in range(HC):
                    r = h % 2
                    op("vector", lambda e, h=h, r=r: e.tensor_sub(
                        fsc[:, r * 512:(r + 1) * 512], xc0[:, h * 512:(h + 1) * 512],
                        bk(4, 0, 512)), waits=[("PE", d0["pe_bcast"])], inc="VE",
                        pub=False)
                    last_ve = op("vector", lambda e, h=h, r=r, tc=tc: e.tensor_mul(
                        ln1_ap(h, tc * 512, 512), fsc[:, r * 512:(r + 1) * 512],
                        bk(5, 0, 512)), inc="VE", pub=False)
            ve_ln1_done = E.fence("vector")
        if "dbg_ln1" in dbg_t:
            for hcx in range(HC):
                op("gpsimd", lambda e, h=hcx: e.dma_start(
                    dbg_t["dbg_ln1"][h * 128:(h + 1) * 128, 0:HALF],
                    lnA[:, h * HALF:(h + 1) * HALF]),
                    waits=[("VE", ve_ln1_done)], inc="sDBG", incv=16)
                op("gpsimd", lambda e, h=hcx: e.dma_start(
                    dbg_t["dbg_ln1"][h * 128:(h + 1) * 128, HALF:S],
                    ln1b[:, h * HALF:(h + 1) * HALF]), inc="sDBG", incv=16)

        Emit.phase = "p1"
        p1 = contextlib.ExitStack()
        e1 = p1.enter_context
        out_all = e1(nc.sbuf_tensor("out_all", [128, NH * S], BF16))
        p1s = contextlib.ExitStack()
        e1s = p1s.enter_context
        qT = e1s(nc.sbuf_tensor("qT", [128, S], BF16))
        kT = e1s(nc.sbuf_tensor("kT", [128, S], BF16))
        vS = e1s(nc.sbuf_tensor("vS", [128, S], BF16))
        frows = e1s(nc.sbuf_tensor("frows", [1, S], BF16))
        frows2 = e1s(nc.sbuf_tensor("frows2", [1, 512], BF16))
        PTs = [e1s(nc.sbuf_tensor(f"PT{i}", [128, 512], BF16)) for i in range(2)]
        cols = e1s(nc.sbuf_tensor("cols", [128, 8], F32))

        sc_head_tail = 0
        ve_bt_tail = 0
        pt_pe = [0, 0]
        pe_qkv_last = 0
        for h in range(NH):
            Emit.phase = "p1qkv"
            sq, isq = ws.load(aw[h].rearrange("p h j -> p (h j)"), HC * 128)
            sk, isk = ws.load(aw[NH + h].rearrange("p h j -> p (h j)"), HC * 128)
            sv, isv = ws.load(aw[2 * NH + h].rearrange("p h j -> p (h j)"), HC * 128)
            sc_evac = {}
            for which, slab, islab, dstT, bias_i in (
                    ("q", sq, isq, qT, h), ("k", sk, isk, kT, NH + h)):
                for tc in range(NTC4):
                    bq = tc % 2
                    w = [ws.ready(islab), ("VE", ve_ln1_done), ("SC", sc_head_tail)]
                    if (which, tc - 2) in sc_evac:
                        w.append(("SC", sc_evac[(which, tc - 2)]))
                    if which == "k":
                        prevq = (S // 512) - 2 + (tc % 2)
                        if ("q", prevq) in sc_evac:
                            w.append(("SC", sc_evac[("q", prevq)]))
                    for hx in range(HC):
                        op("tensor", lambda e, s=slab, hx=hx, tc=tc, bq=bq: e.matmul(
                            bk(bq), ws.wsb[s][:, hx * 128:(hx + 1) * 128],
                            ln1_ap(hx, tc * 512, 512),
                            start=(hx == 0), stop=(hx == HC - 1)),
                            waits=(w if hx == 0 else []),
                            inc=("PE" if hx == HC - 1 else None))
                    pe_g = now("PE")
                    sc_evac[(which, tc)] = op(
                        "scalar", lambda e, d=dstT, tc=tc, bq=bq, bi=bias_i: e.activation(
                            d[:, tc * 512:(tc + 1) * 512], bk(bq), AF.Identity,
                            bias=abs_[:, bi:bi + 1]),
                        waits=[("PE", pe_g)], inc="SC")
            ws.done(isq, now("PE"))
            ws.done(isk, now("PE"))
            for tt in range(NT):
                bv = 2 + (tt // 4) % 2
                reg = tt % 4
                w = [ws.ready(isv), ("VE", ve_ln1_done), ("SC", sc_head_tail)]
                if ("v", tt // 4 - 2) in sc_evac:
                    w.append(("SC", sc_evac[("v", tt // 4 - 2)]))
                for hx in range(HC):
                    op("tensor", lambda e, hx=hx, tt=tt, bv=bv, reg=reg, s=sv: e.matmul(
                        bk(bv, reg * 128, reg * 128 + 128),
                        ln1_ap(hx, tt * 128, 128),
                        ws.wsb[s][:, hx * 128:(hx + 1) * 128],
                        start=(hx == 0), stop=(hx == HC - 1)),
                        waits=(w if hx == 0 else []),
                        inc=("PE" if hx == HC - 1 else None))
                if reg == 3:
                    pe_g = now("PE")
                    sc_evac[("v", tt // 4)] = op(
                        "scalar", lambda e, tt=tt, bv=bv: e.copy(
                            vS[:, (tt - 3) * 128:(tt + 1) * 128], bk(bv)),
                        waits=[("PE", pe_g)], inc="SC")
            ws.done(isv, now("PE"))
            sc_qkv_done = sc_evac[("v", NT // 4 - 1)]
            pe_qkv_last = now("PE")

            Emit.phase = "p1a"
            sc_frows = {}
            ve_max_v = {}
            pend = None
            for qi in range(NT):
                buf = qi % 2
                base = buf * 2048
                kw = (qi + 1) * 128
                r = qi % 4
                w = [("SC", sc_qkv_done), ("dW", d_gw), ("VE", ve_bt_tail)]
                if qi - 2 in ve_max_v:
                    w.append(("VE", ve_max_v[qi - 2]))
                offs = list(range(0, kw, 512))[::-1]
                first = True
                for off in offs:
                    wd = min(512, kw - off)
                    wc = list(w) if first else []
                    if off == 0:
                        wc = wc + [("SC", sc_frows.get(qi - 2, 0))]
                    op("tensor", lambda e, qi=qi, off=off, wd=wd, base=base: e.matmul(
                        PS[:, base + off:base + off + wd],
                        qT[:, qi * 128:(qi + 1) * 128], kT[:, off:off + wd],
                        start=True, stop=True),
                        waits=wc,
                        inc=("PE" if off == 0 else None))
                    first = False
                pe_row = now("PE")
                op("vector", lambda e, qi=qi, base=base, kw=kw: e.tensor_tensor(
                    PS[:, base + qi * 128:base + kw], PS[:, base + qi * 128:base + kw],
                    cmask[:], ALU.add), waits=[("PE", pe_row)], inc="VE", pub=False)
                ve_mx = op("vector", lambda e, base=base, kw=kw, r=r: e.reduce_max(
                    cols[:, r:r + 1], PS[:, base:base + kw], axis=AX, negate=True),
                    inc="VE")
                ve_max_v[qi] = ve_mx
                if pend is not None:
                    pqi, pr, pvf = pend
                    tbase = (pqi % 2) * 2048
                    pe_tr = op("tensor", lambda e, tb=tbase, r=pr: e.transpose(
                        PS[0:1, tb:tb + 128], cols[:, r:r + 1], ident[:]),
                        waits=[("VE", pvf), ("SC", sc_frows.get(pqi - 2, 0))],
                        inc="PE")
                    sc_frows[pqi] = op("scalar", lambda e, tb=tbase, qi=pqi: e.copy(
                        frows[0:1, qi * 128:(qi + 1) * 128], PS[0:1, tb:tb + 128]),
                        waits=[("PE", pe_tr)], inc="SC")
                pend = (qi, r, ve_mx)
            pqi, pr, pvf = pend
            tbase = (pqi % 2) * 2048
            pe_tr = op("tensor", lambda e, tb=tbase, r=pr: e.transpose(
                PS[0:1, tb:tb + 128], cols[:, r:r + 1], ident[:]),
                waits=[("VE", pvf)], inc="PE")
            sc_frows[pqi] = op("scalar", lambda e, tb=tbase, qi=pqi: e.copy(
                frows[0:1, qi * 128:(qi + 1) * 128], PS[0:1, tb:tb + 128]),
                waits=[("PE", pe_tr)], inc="SC")
            sc_pa_done = sc_frows[NT - 1]

            Emit.phase = "p1b"
            gslot = 0
            stp_sc = [0, 0]
            sc_outev = 0
            ve_outev = 0
            sc_rb = 0
            pe_pv = 0
            for qi in range(NT):
                nk = qi + 1
                for g0 in range(0, nk, 4):
                    gwd = min(4, nk - g0)
                    slot = gslot % 2
                    gslot += 1
                    base = (4 + slot) * 512
                    w = [("SC", sc_pa_done), ("SC", stp_sc[slot]), ("SC", sc_rb)]
                    first = True
                    for j in range(gwd):
                        kc = g0 + j
                        op("tensor", lambda e, j=j, kc=kc, qi=qi, base=base: e.matmul(
                            PS[:, base + j * 128:base + (j + 1) * 128],
                            kT[:, kc * 128:(kc + 1) * 128],
                            qT[:, qi * 128:(qi + 1) * 128],
                            start=True, stop=False),
                            waits=(w if first else []), inc=None)
                        first = False
                        op("tensor", lambda e, j=j, qi=qi, base=base: e.matmul(
                            PS[:, base + j * 128:base + (j + 1) * 128],
                            ones1b[:], frows[0:1, qi * 128:(qi + 1) * 128],
                            start=False, stop=True),
                            inc=("PE" if j == gwd - 1 else None))
                    pe_grp = now("PE")
                    wexp = [("PE", pe_grp), ("PE", pt_pe[slot])]
                    if g0 <= qi < g0 + gwd:
                        jd = qi - g0
                        ve_d = op("vector", lambda e, jd=jd, base=base: e.tensor_tensor(
                            PS[:, base + jd * 128:base + (jd + 1) * 128],
                            PS[:, base + jd * 128:base + (jd + 1) * 128],
                            cmaskT[:], ALU.add), waits=[("PE", pe_grp)], inc="VE")
                        wexp.append(("VE", ve_d))
                    sc_e = op("scalar", lambda e, slot=slot, base=base, gwd=gwd: e.activation(
                        PTs[slot][:, 0:gwd * 128], PS[:, base:base + gwd * 128],
                        AF.Exp, scale=AL), waits=wexp, inc="SC")
                    stp_sc[slot] = sc_e
                    wpv = [("SC", sc_e), ("SC", sc_outev), ("VE", ve_outev)]
                    for j in range(gwd):
                        kc = g0 + j
                        op("tensor", lambda e, j=j, kc=kc, qi=qi, slot=slot: e.matmul(
                            bk(6, (qi % 4) * 128, (qi % 4) * 128 + 128),
                            vS[:, kc * 128:(kc + 1) * 128],
                            PTs[slot][:, j * 128:(j + 1) * 128],
                            start=(kc == 0), stop=(kc == qi)),
                            waits=(wpv if j == 0 else []), inc=None)
                        op("tensor", lambda e, j=j, kc=kc, qi=qi, slot=slot: e.matmul(
                            PS[0:1, 3584 + (qi % 4) * 128:3584 + (qi % 4) * 128 + 128],
                            onescb[:], PTs[slot][:, j * 128:(j + 1) * 128],
                            start=(kc == 0), stop=(kc == qi)),
                            inc=("PE" if kc == qi else None))
                    pt_pe[slot] = now("PE")
                pe_pv = now("PE")
                if qi % 4 == 3:
                    op("vector", lambda e: e.reciprocal(
                        rrow[:, 0:512], PS[0:1, 3584:3584 + 512]),
                        waits=[("PE", pe_pv)], inc="VE")
                    ve_rc = op("vector", lambda e: e.tensor_copy(
                        frows2[0:1, 0:512], rrow[:, 0:512]), inc="VE")
                    pe_rb = op("tensor", lambda e: e.matmul(
                        PS[:, 3584:3584 + 512], ones1b[:], frows2[0:1, 0:512],
                        start=True, stop=True), waits=[("VE", ve_rc)], inc="PE")
                    sc_rb = op("scalar", lambda e: e.copy(
                        fsc[:, 512:1024], PS[:, 3584:3584 + 512]),
                        waits=[("PE", pe_rb)], inc="SC")
                    ve_outev = op("vector", lambda e: e.tensor_mul(
                        fsc[:, 0:512], bk(6), fsc[:, 512:1024]),
                        waits=[("SC", sc_rb), ("SC", sc_outev)], inc="VE")
                    sc_outev = op("scalar", lambda e, h=h, qi=qi: e.activation(
                        out_all[:, h * S + (qi - 3) * 128:h * S + (qi + 1) * 128],
                        fsc[:, 0:512], AF.Identity,
                        bias=abs_[:, 2 * NH + h:2 * NH + h + 1]),
                        waits=[("VE", ve_outev)], inc="SC")
            sc_head_tail = sc_outev
            ve_bt_tail = ve_outev
        sc_attn_done = sc_head_tail
        pe_ln1_free = pe_qkv_last
        if "dbg_attn" in dbg_t:
            op("gpsimd", lambda e: e.dma_start(dbg_t["dbg_attn"], out_all[:]),
               waits=[("SC", sc_attn_done)], inc="sDBG", incv=16)

        Emit.phase = "p2a"
        p1s.close()
        asb = e1(nc.sbuf_tensor("asb", [128, 2 * 512], F32))
        asb_hist = {}
        for dac in range(HC):
            sp, isp = ws.load(pw[dac].rearrange("p h j -> p (h j)"), HC * 128)
            for tcw in range(NTC4):
                gi = dac * NTC4 + tcw
                bq = tcw % 2
                w = [ws.ready(isp), ("SC", sc_attn_done)]
                if gi - 2 in asb_hist:
                    w.append(("SC", asb_hist[gi - 2][0]))
                for dcx in range(HC):
                    op("tensor", lambda e, s=sp, dcx=dcx, tcw=tcw, bq=bq: e.matmul(
                        bk(bq), ws.wsb[s][:, dcx * 128:(dcx + 1) * 128],
                        out_all[:, dcx * S + tcw * 512:dcx * S + (tcw + 1) * 512],
                        start=(dcx == 0), stop=(dcx == HC - 1)),
                        waits=(w if dcx == 0 else []),
                        inc=("PE" if dcx == HC - 1 else None))
                pe_g = now("PE")
                ring = gi % 2
                wsc = [("PE", pe_g)]
                if gi - 2 in asb_hist:
                    wsc.append((f"sASB{ring}", asb_hist[gi - 2][1]))
                sc_a = op("scalar", lambda e, ring=ring, bq=bq, dac=dac: e.activation(
                    asb[:, ring * 512:(ring + 1) * 512], bk(bq), AF.Identity,
                    bias=pbs[:, dac:dac + 1]), waits=wsc, inc="SC")
                dxv = op("gpsimd", lambda e, ring=ring, dac=dac, tcw=tcw: e.dma_start(
                    aTs[dac * 128:(dac + 1) * 128, tcw * 512:(tcw + 1) * 512],
                    asb[:, ring * 512:(ring + 1) * 512]),
                    waits=[("SC", sc_a)], inc=f"sASB{ring}", incv=16)
                asb_hist[gi] = (sc_a, dxv)
            ws.done(isp, now("PE"))
        p2a_done = [("sASB0", E.cnt.get("sASB0", 0)), ("sASB1", E.cnt.get("sASB1", 0))]
        if "dbg_a" in dbg_t:
            op("gpsimd", lambda e: e.dma_start(dbg_t["dbg_a"], aTs),
               waits=p2a_done, inc="sDBG", incv=16)
        p1.close()
        ln1_stack.close()

        Emit.phase = "p2b"
        ve_lnA_w = 0
        dx_x1_done = 0
        dx_comb_done = 0
        p2 = contextlib.ExitStack()
        e2 = p2.enter_context
        ac = e2(nc.sbuf_tensor("ac", [128, HC * 128], F32))
        xc2 = e2(nc.sbuf_tensor("xc2", [128, HC * 128], F32))
        lnt = e2(nc.sbuf_tensor("lnt", [128, HC * 128], BF16))
        gl = e2(nc.sbuf_tensor("gl", [128, NE], F32))
        gp = e2(nc.sbuf_tensor("gp", [128, NE], F32))
        gwk = e2(nc.sbuf_tensor("gwk", [128, NE], F32))
        gm = e2(nc.sbuf_tensor("gm", [128, 4], F32))
        r2c = e2(nc.sbuf_tensor("r2c", [128, 1], F32))
        ctile = e2(nc.sbuf_tensor("ctile", [NE, 128], F32))
        n2row = e2(nc.sbuf_tensor("n2row", [1, 128], F32))

        prev_ct_sc = 0
        prev_pe_pg = 0
        dx_lnt_prev = 0
        dx_x1 = 0
        for tc in range(NT):
            wdma = list(p2a_done)
            if tc > 0:
                wdma += [("VE", now("VE")), ("SC", now("SC")), ("PE", now("PE"))]
            dva = op("gpsimd", lambda e, tc=tc: e.dma_start(
                ac[:].rearrange("p (h t) -> p h t", h=HC),
                aTs[:, tc * 128:(tc + 1) * 128].rearrange("(h p) t -> p h t", p=128)),
                waits=wdma, inc="sAC", incv=16)
            wdmx = [("sX1", dx_x1)] if tc > 0 else []
            dvx = op("gpsimd", lambda e, tc=tc: e.dma_start(
                xc2[:].rearrange("p (h t) -> p h t", h=HC),
                xT[:, tc * 128:(tc + 1) * 128].rearrange("(h p) t -> p h t", p=128)),
                waits=wdmx, inc="sXC2", incv=16)
            d1 = emit_ln_stats(lambda hx: ac[:, hx * 128:(hx + 1) * 128], 128,
                               2, 3, 4, 5, [("sAC", dva)])
            last_ve = 0
            for hx in range(HC):
                r = hx % 2
                w = [("PE", d1["pe_bcast"])] + ([("sXC2", dvx)] if hx == 0 else [])
                op("vector", lambda e, hx=hx, r=r: e.tensor_sub(
                    fsc[:, r * 512:r * 512 + 128], ac[:, hx * 128:(hx + 1) * 128],
                    bk(4, 0, 128)), waits=w, inc="VE", pub=False)
                op("vector", lambda e, r=r: e.tensor_mul(
                    fsc[:, r * 512:r * 512 + 128], fsc[:, r * 512:r * 512 + 128],
                    bk(5, 0, 128)), inc="VE", pub=False)
                last_ve = op("vector", lambda e, hx=hx, r=r: e.tensor_add(
                    xc2[:, hx * 128:(hx + 1) * 128], xc2[:, hx * 128:(hx + 1) * 128],
                    fsc[:, r * 512:r * 512 + 128]), inc="VE",
                    pub=(hx == HC - 1))
            dx_x1 = op("gpsimd", lambda e, tc=tc: e.dma_start(
                x1T[:, tc * 128:(tc + 1) * 128].rearrange("(h p) t -> p h t", p=128),
                xc2[:].rearrange("p (h t) -> p h t", h=HC)),
                waits=[("VE", last_ve)], inc="sX1", incv=16)
            dx_x1_done = dx_x1
            d2 = emit_ln_stats(lambda hx: xc2[:, hx * 128:(hx + 1) * 128], 128,
                               2, 3, 6, 7, [("VE", last_ve)])
            last_ve2 = 0
            half1 = tc * 128 >= HALF
            for hx in range(HC):
                r = hx % 2
                w = [("PE", d2["pe_bcast"])]
                if hx == 0:
                    if half1 and dx_lnt_prev:
                        w.append(("sLNT", dx_lnt_prev))
                    if not half1 and tc == 0:
                        w.append(("PE", pe_ln1_free))
                op("vector", lambda e, hx=hx, r=r: e.tensor_sub(
                    fsc[:, r * 512:r * 512 + 128], xc2[:, hx * 128:(hx + 1) * 128],
                    bk(6, 0, 128)), waits=w, inc="VE", pub=False)
                dst = (lnt[:, hx * 128:(hx + 1) * 128] if half1 else
                       lnA[:, hx * HALF + tc * 128:hx * HALF + (tc + 1) * 128])
                last_ve2 = op("vector", lambda e, d=dst, r=r: e.tensor_mul(
                    d, fsc[:, r * 512:r * 512 + 128], bk(7, 0, 128)), inc="VE",
                    pub=(hx == HC - 1))
            if half1:
                dx_lnt_prev = op("gpsimd", lambda e, tc=tc: e.dma_start(
                    ln2s[:, tc * 128:(tc + 1) * 128].rearrange("(h p) t -> p h t", p=128),
                    lnt[:].rearrange("p (h t) -> p h t", h=HC)),
                    waits=[("VE", last_ve2)], inc="sLNT", incv=16)
            ve_lnA_w = max(ve_lnA_w, last_ve2)
            for hx in range(HC):
                w = [("VE", last_ve), ("VE", d2["ve_m"])] if hx == 0 else []
                op("tensor", lambda e, hx=hx: e.matmul(
                    bk(3, 256, 256 + NE), xc2[:, hx * 128:(hx + 1) * 128],
                    gws[:, hx * NE:(hx + 1) * NE], start=(hx == 0), stop=False),
                    waits=w, inc=None)
            ve_n2 = op("vector", lambda e: e.tensor_scalar(
                n2row[:], mrow[:, 0:128], -1.0, None, ALU.mult),
                waits=[("VE", d2["ve_m"]), ("PE", prev_pe_pg)], inc="VE")
            pe_pg = op("tensor", lambda e: e.matmul(
                bk(3, 256, 256 + NE), n2row[:], gsum[:], start=False, stop=True),
                waits=[("VE", ve_n2), ("SC", sc_gsum)], inc="PE")
            prev_pe_pg = pe_pg
            pe_r2 = op("tensor", lambda e: e.transpose(
                bk(1, 0, 1), rrow[0:1, 0:128], ident[0:1, 0:1]),
                waits=[("SC", d2["sc_r"]), ("SC", prev_ct_sc)], inc="PE")
            sc_r2 = op("scalar", lambda e: e.copy(r2c[:], bk(1, 0, 1)),
                       waits=[("PE", pe_r2)], inc="SC")
            op("vector", lambda e: e.tensor_scalar(
                gl[:], bk(3, 256, 256 + NE), r2c[:], None, ALU.mult),
                waits=[("PE", pe_pg), ("SC", sc_r2)], inc="VE", pub=False)
            ve_gm = op("vector", lambda e: e.reduce_max(gm[:, 0:1], gl[:], axis=AX,
                                                        negate=True), inc="VE")
            sc_gp = op("scalar", lambda e: e.activation(
                gp[:], gl[:], AF.Exp, bias=gm[:, 0:1], accum_out=gm[:, 1:2]),
                waits=[("VE", ve_gm)], inc="SC")
            op("vector", lambda e: e.reciprocal(gm[:, 2:3], gm[:, 1:2]),
               waits=[("SC", sc_gp)], inc="VE")
            op("vector", lambda e: e.tensor_scalar(
                gp[:], gp[:], gm[:, 2:3], None, ALU.mult), inc="VE")
            op("vector", lambda e: e.tensor_copy(gwk[:], gp[:]), inc="VE")
            for _ in range(c.TOPK):
                op("vector", lambda e: e.reduce_max(gm[:, 3:4], gwk[:], axis=AX),
                   inc="VE")
                op("vector", lambda e: e.tensor_single_scalar(
                    gl[:], gwk[:], gm[:, 3:4], ALU.is_ge), inc="VE")
                op("vector", lambda e: e.scalar_tensor_tensor(
                    gwk[:], gl[:], -2.0, gwk[:], ALU.mult, ALU.add), inc="VE")
            op("vector", lambda e: e.tensor_sub(gl[:], gp[:], gwk[:]), inc="VE")
            op("vector", lambda e: e.scalar_tensor_tensor(
                gl[:], gl[:], 0.5, gp[:], ALU.mult, ALU.mult), inc="VE")
            op("vector", lambda e: e.reduce_sum(gm[:, 3:4], gl[:], axis=AX), inc="VE")
            op("vector", lambda e: e.tensor_scalar(
                gm[:, 3:4], gm[:, 3:4], 1e-20, None, ALU.add), inc="VE")
            op("vector", lambda e: e.reciprocal(gm[:, 2:3], gm[:, 3:4]), inc="VE")
            ve_cb = op("vector", lambda e: e.tensor_scalar(
                gl[:], gl[:], gm[:, 2:3], None, ALU.mult), inc="VE")
            pe_ct = op("tensor", lambda e: e.transpose(
                bkr(1, 128, 256, 0, NE), gl[:, 0:NE], ident[:]),
                waits=[("VE", ve_cb)], inc="PE")
            sc_ct = op("scalar", lambda e: e.copy(ctile[:], bkr(1, 128, 256, 0, NE)),
                       waits=[("PE", pe_ct), ("sCT", dx_comb_done)], inc="SC")
            dx_ct = op("gpsimd", lambda e, tc=tc: e.dma_start(
                combTd[:, tc * 128:(tc + 1) * 128], ctile[:]),
                waits=[("SC", sc_ct)], inc="sCT", incv=16)
            prev_ct_sc = sc_ct
            dx_comb_done = dx_ct
        ve_p2_done = now("VE")
        sc_p2_done = now("SC")
        p2.close()
        if "dbg_x1" in dbg_t:
            op("gpsimd", lambda e: e.dma_start(dbg_t["dbg_x1"], x1T),
               waits=[("sX1", dx_x1_done)], inc="sDBG", incv=16)
        if "dbg_ln2" in dbg_t:
            for hcx in range(HC):
                op("gpsimd", lambda e, hx=hcx: e.dma_start(
                    dbg_t["dbg_ln2"][hx * 128:(hx + 1) * 128, 0:HALF],
                    lnA[:, hx * HALF:(hx + 1) * HALF]),
                    waits=[("VE", ve_p2_done)], inc="sDBG", incv=16)
            op("gpsimd", lambda e: e.dma_start(
                dbg_t["dbg_ln2"][:, HALF:S], ln2s[:, HALF:S]),
                waits=[("sLNT", dx_lnt_prev)], inc="sDBG", incv=16)
        if "dbg_comb" in dbg_t:
            op("gpsimd", lambda e: e.dma_start(dbg_t["dbg_comb"], combTd),
               waits=[("sCT", dx_comb_done)], inc="sDBG", incv=16)

        Emit.phase = "p3"
        acc = en(nc.sbuf_tensor("acc", [128, HC * HALF], F32, side="right"))
        p3 = contextlib.ExitStack()
        e3 = p3.enter_context

        def w13_slab(ei, jc):
            return w13[ei, jc].rearrange("p h j -> p (h j)")

        def sw13_slab(jc):
            return sw13[jc].rearrange("p h j -> p (h j)")

        bank_sc = {}
        bank_ve = {}
        sgb_ve = {}
        usb_ve = {}
        crow_pe = 0
        cbs_ve_free = [0, 0]
        acc_read_ve = 0
        lnA_pe_last = 0
        ve_out_done = 0
        dx_out = 0
        lnA_ready = ("VE", ve_lnA_w)

        for half in range(2):
            t0 = half * HALF
            half_entry = [(s, E.cnt[s]) for s in list(E.cnt)]
            p3h = contextlib.ExitStack()
            e3h = p3h.enter_context
            act0 = e3h(nc.sbuf_tensor(f"act0_{half}", [128, J2 * HALF], BF16))
            act1 = e3h(nc.sbuf_tensor(f"act1_{half}", [128, J2 * HALF], BF16))
            acts = [act0, act1]
            cbs = [e3h(nc.sbuf_tensor(f"cbs{i}_{half}", [128, HALF], BF16)) for i in range(2)]
            crow = e3h(nc.sbuf_tensor(f"crow_{half}", [1, HALF], BF16))
            sgb = [e3h(nc.sbuf_tensor(f"sgb{i}_{half}", [128, 512], BF16)) for i in range(2)]
            usb = [e3h(nc.sbuf_tensor(f"usb{i}_{half}", [128, 512], BF16)) for i in range(2)]
            if half == 1:
                dvr = op("gpsimd", lambda e: e.dma_start(
                    lnA[:].rearrange("p (h t) -> p h t", h=HC),
                    ln2s[:, HALF:S].rearrange("(h p) t -> p h t", p=128)),
                    waits=[("PE", lnA_pe_last), ("sLNT", dx_lnt_prev)],
                    inc="sLNR", incv=16)
                lnA_ready = ("sLNR", dvr)
            pairs = [(2 * p, 2 * p + 1) for p in range(NE // 2)] + [("shA", "shB")]
            prev_act_pe = 0
            for ip, (ea, eb) in enumerate(pairs):
                shared = ea == "shA"
                if not shared:
                    sc_cb_last = 0
                    for im, ei in enumerate((ea, eb)):
                        dvc = op("gpsimd", lambda e, ei=ei, t0=t0, cr_=crow: e.dma_start(
                            cr_[:, 0:HALF], combTd[ei:ei + 1, t0:t0 + HALF]),
                            waits=[("sCT", dx_comb_done), ("PE", crow_pe)] + half_entry,
                            inc="sCROW", incv=16)
                        for tq in range(HALF // 512):
                            wcb = [("sCROW", dvc), ("SC", bank_sc.get(6, 0)),
                                   ("VE", bank_ve.get(6, 0)),
                                   ("VE", cbs_ve_free[im])] + half_entry
                            op("tensor", lambda e, tq=tq, cr_=crow: e.matmul(
                                bk(6), ones1b[:], cr_[0:1, tq * 512:(tq + 1) * 512],
                                start=True, stop=True), waits=wcb, inc="PE")
                            crow_pe = now("PE")
                            sc_cb_last = op("scalar", lambda e, tq=tq, cb_=cbs[im]: e.copy(
                                cb_[:, tq * 512:(tq + 1) * 512], bk(6)),
                                waits=[("PE", crow_pe)], inc="SC")
                            bank_sc[6] = sc_cb_last
                    cb_ready = ("SC", sc_cb_last)
                else:
                    cb_ready = ("SC", 0)
                njc = J2
                ve_act_last = 0
                for jc in range(njc):
                    slab_src = ([(sw13_slab(jc), sw13_slab(JS2 + jc)),
                                 (sw13_slab(njc + jc), sw13_slab(JS2 + njc + jc))]
                                if shared else
                                [(w13_slab(ea, jc), w13_slab(ea, J2 + jc)),
                                 (w13_slab(eb, jc), w13_slab(eb, J2 + jc))])
                    slabs = []
                    for im in range(2):
                        gsrc, usrc = slab_src[im]
                        sgl, isg = ws.load(gsrc, HC * 128)
                        sul, isu = ws.load(usrc, HC * 128)
                        for tq in range(HALF // 512):
                            bg, bu = (0, 1) if tq == 0 else (2, 3)
                            ring = tq
                            wg = [ws.ready(isg), lnA_ready,
                                  ("SC", bank_sc.get(bg, 0)),
                                  ("VE", bank_ve.get(bg, 0))] + half_entry
                            for hx in range(HC):
                                op("tensor", lambda e, s=sgl, hx=hx, tq=tq, bg=bg: e.matmul(
                                    bk(bg), ws.wsb[s][:, hx * 128:(hx + 1) * 128],
                                    lnA[:, hx * HALF + tq * 512:hx * HALF + (tq + 1) * 512],
                                    start=(hx == 0), stop=(hx == HC - 1)),
                                    waits=(wg if hx == 0 else []),
                                    inc=("PE" if hx == HC - 1 else None))
                            pe_g = now("PE")
                            wu = [ws.ready(isu), ("VE", bank_ve.get(bu, 0)),
                                  ("SC", bank_sc.get(bu, 0))]
                            for hx in range(HC):
                                op("tensor", lambda e, s=sul, hx=hx, tq=tq, bu=bu: e.matmul(
                                    bk(bu), ws.wsb[s][:, hx * 128:(hx + 1) * 128],
                                    lnA[:, hx * HALF + tq * 512:hx * HALF + (tq + 1) * 512],
                                    start=(hx == 0), stop=(hx == HC - 1)),
                                    waits=(wu if hx == 0 else []),
                                    inc=("PE" if hx == HC - 1 else None))
                            pe_u = now("PE")
                            lnA_pe_last = pe_u
                            sc_sg = op("scalar", lambda e, bg=bg, sg_=sgb[ring]: e.activation(
                                sg_[:], bk(bg), AF.Silu),
                                waits=[("PE", pe_g), ("VE", sgb_ve.get(ring, 0))]
                                + half_entry, inc="SC")
                            bank_sc[bg] = sc_sg
                            wuc = [("PE", pe_u), ("VE", usb_ve.get(ring, 0))] + half_entry
                            if shared:
                                ve_uc = op("vector", lambda e, bu=bu, ub_=usb[ring]: e.tensor_copy(
                                    ub_[:], bk(bu)), waits=wuc, inc="VE",
                                    pub=False)
                            else:
                                wuc.append(cb_ready)
                                ve_uc = op("vector",
                                           lambda e, bu=bu, tq=tq, ub_=usb[ring],
                                           cb_=cbs[im]:
                                           e.tensor_tensor(
                                               ub_[:], bk(bu),
                                               cb_[:, tq * 512:(tq + 1) * 512],
                                               ALU.mult), waits=wuc, inc="VE",
                                           pub=False)
                            bank_ve[bu] = ve_uc
                            dsta = acts[im][:, jc * HALF + tq * 512:
                                            jc * HALF + (tq + 1) * 512]
                            wact = [("SC", sc_sg)] + half_entry
                            if jc == 0 and tq == 0:
                                wact.append(("PE", prev_act_pe))
                            ve_a = op("vector", lambda e, d=dsta, sg_=sgb[ring],
                                      ub_=usb[ring]: e.tensor_mul(
                                d, sg_[:], ub_[:]), waits=wact, inc="VE")
                            sgb_ve[ring] = ve_a
                            usb_ve[ring] = ve_a
                            ve_act_last = ve_a
                        ws.done(isg, now("PE"))
                        ws.done(isu, now("PE"))
                if not shared:
                    cbs_ve_free = [ve_act_last, ve_act_last]
                nyj = J2
                for hx in range(HC):
                    if shared:
                        sy_a, isy_a = ws.load(
                            sw2[hx, :, 0:JS2 // 2, :].rearrange("p j k -> p (j k)"),
                            (JS2 // 2) * 128)
                        sy_b, isy_b = ws.load(
                            sw2[hx, :, JS2 // 2:JS2, :].rearrange("p j k -> p (j k)"),
                            (JS2 // 2) * 128)
                    else:
                        sy_a, isy_a = ws.load(w2[ea, hx].rearrange("p j k -> p (j k)"),
                                              J2 * 128)
                        sy_b, isy_b = ws.load(w2[eb, hx].rearrange("p j k -> p (j k)"),
                                              J2 * 128)
                    for tq in range(HALF // 512):
                        by = 4 + 2 * tq + (hx & 1)
                        wy = [ws.ready(isy_a), ws.ready(isy_b), ("VE", ve_act_last),
                              ("VE", bank_ve.get(by, 0)), ("SC", bank_sc.get(by, 0))]
                        first = True
                        for im, syl in ((0, sy_a), (1, sy_b)):
                            for j2x in range(nyj):
                                lastmm = (im == 1 and j2x == nyj - 1)
                                op("tensor",
                                   lambda e, s=syl, j2x=j2x, im=im, tq=tq, by=by,
                                   lm=lastmm, a_=acts[im]: e.matmul(
                                       bk(by), ws.wsb[s][:, j2x * 128:(j2x + 1) * 128],
                                       a_[:, j2x * HALF + tq * 512:
                                          j2x * HALF + (tq + 1) * 512],
                                       start=(im == 0 and j2x == 0), stop=lm),
                                   waits=(wy if first else []),
                                   inc=("PE" if lastmm else None))
                                first = False
                        pe_y = now("PE")
                        dsta = acc[:, hx * HALF + tq * 512:hx * HALF + (tq + 1) * 512]
                        if ip == 0:
                            v = op("scalar", lambda e, d=dsta, by=by: e.copy(d, bk(by)),
                                   waits=[("PE", pe_y), ("VE", acc_read_ve)], inc="SC")
                            bank_sc[by] = v
                        else:
                            v = op("vector", lambda e, d=dsta, by=by: e.tensor_tensor(
                                d, bk(by), d, ALU.add), waits=[("PE", pe_y)], inc="VE",
                                pub=False)
                            bank_ve[by] = v
                    ws.done(isy_a, now("PE"))
                    ws.done(isy_b, now("PE"))
                prev_act_pe = now("PE")
            ve_acc_done = E.fence("vector")
            sc_acc_done = now("SC")
            if half == 1 and "dbg_acc1" in dbg_t:
                for hcx in range(HC):
                    op("gpsimd", lambda e, hx=hcx: e.dma_start(
                        dbg_t["dbg_acc1"][hx * 128:(hx + 1) * 128, :],
                        acc[:, hx * HALF:(hx + 1) * HALF]),
                        waits=[("VE", ve_acc_done), ("SC", sc_acc_done)],
                        inc="sDBG", incv=16)
            if half == 1 and "dbg_lnr" in dbg_t:
                op("gpsimd", lambda e: e.dma_start(
                    dbg_t["dbg_lnr"][:, 0:HC * HALF], lnA[:]),
                    waits=[("VE", ve_acc_done)], inc="sDBG", incv=16)
            if half == 0 and "dbg_acc" in dbg_t:
                for hcx in range(HC):
                    op("gpsimd", lambda e, hx=hcx: e.dma_start(
                        dbg_t["dbg_acc"][hx * 128:(hx + 1) * 128, :],
                        acc[:, hx * HALF:(hx + 1) * HALF]),
                        waits=[("VE", ve_acc_done), ("SC", sc_acc_done)],
                        inc="sDBG", incv=16)

            p3h.close()
            p4h = contextlib.ExitStack()
            xpc = p4h.enter_context(nc.sbuf_tensor(f"xpc_{half}", [128, HC * 128], F32))
            p4_entry = [(s, E.cnt[s]) for s in list(E.cnt)]
            for t4 in range(HALF // 128):
                wdx = [("sX1", dx_x1_done)] + (p4_entry if t4 == 0 else [])
                if dx_out:
                    wdx += [("VE", ve_out_done), ("sOUT", dx_out)]
                dvx = op("gpsimd", lambda e, t4=t4, t0=t0, xp_=xpc: e.dma_start(
                    xp_[:].rearrange("p (h t) -> p h t", h=HC),
                    x1T[:, t0 + t4 * 128:t0 + (t4 + 1) * 128]
                    .rearrange("(h p) t -> p h t", p=128)),
                    waits=wdx, inc="sXPC", incv=16)
                d4 = emit_ln_stats(
                    lambda hx, t4=t4: acc[:, hx * HALF + t4 * 128:
                                          hx * HALF + t4 * 128 + 128],
                    128, 2, 3, 4, 5,
                    [("VE", ve_acc_done), ("SC", sc_acc_done)])
                last_ve = 0
                for hx in range(HC):
                    r = hx % 2
                    w = [("PE", d4["pe_bcast"])] + ([("sXPC", dvx)] if hx == 0 else [])
                    op("vector", lambda e, hx=hx, r=r, t4=t4: e.tensor_sub(
                        fsc[:, r * 512:r * 512 + 128],
                        acc[:, hx * HALF + t4 * 128:hx * HALF + t4 * 128 + 128],
                        bk(4, 0, 128)), waits=w, inc="VE", pub=False)
                    op("vector", lambda e, r=r: e.tensor_mul(
                        fsc[:, r * 512:r * 512 + 128], fsc[:, r * 512:r * 512 + 128],
                        bk(5, 0, 128)), inc="VE", pub=False)
                    last_ve = op("vector", lambda e, hx=hx, r=r, xp_=xpc: e.tensor_add(
                        xp_[:, hx * 128:(hx + 1) * 128], xp_[:, hx * 128:(hx + 1) * 128],
                        fsc[:, r * 512:r * 512 + 128]), inc="VE",
                        pub=(hx == HC - 1))
                ve_out_done = last_ve
                acc_read_ve = now("VE")
                dx_out = op("gpsimd", lambda e, t4=t4, t0=t0, xp_=xpc: e.dma_start(
                    outT[:, t0 + t4 * 128:t0 + (t4 + 1) * 128]
                    .rearrange("(h p) t -> p h t", p=128),
                    xp_[:].rearrange("p (h t) -> p h t", h=HC)),
                    waits=[("VE", last_ve)], inc="sOUT", incv=16)
            for bnk in range(6):
                bank_ve[bnk] = max(bank_ve.get(bnk, 0), now("VE"))
            p4h.close()
        fin = [("sOUT", E.cnt.get("sOUT", 0))]
        if dbg_t:
            fin.append(("sDBG", E.cnt.get("sDBG", 0)))
        op("gpsimd", lambda e: e.engine_nop(), waits=fin, inc=None)
        p3.close()

        block = en(nc.Block())
        E.flush(block)
    return nc


def prep_shared(c, c_attn_w, c_attn_b, c_proj_w, c_proj_b, gate_w, w13, w2, sw13, sw2):
    HC, NE, J2, JS2 = c.HC, c.NE, c.J2, c.JS2
    f = np.ascontiguousarray
    aw = f(c_attn_w.reshape(HC, 128, 3 * HC, 128).transpose(2, 1, 0, 3))
    ab = f(c_attn_b.reshape(3 * HC, 128).T)
    pw = f(c_proj_w.reshape(HC, 128, HC, 128).transpose(2, 1, 0, 3))
    pb = f(c_proj_b.reshape(HC, 128).T)
    gw = f(gate_w.T.reshape(HC, 128, NE))
    w13b = f(w13.reshape(NE, 2 * J2, 128, HC, 128).transpose(0, 1, 4, 3, 2))
    w2b = f(w2.reshape(NE, HC, 128, J2, 128).transpose(0, 1, 4, 3, 2))
    sw13b = f(sw13.reshape(2 * JS2, 128, HC, 128).transpose(0, 3, 2, 1))
    sw2b = f(sw2.reshape(HC, 128, JS2, 128).transpose(0, 3, 2, 1))
    ident = np.eye(128, dtype=np.float32)
    ii, jj = np.meshgrid(np.arange(128), np.arange(128), indexing="ij")
    cmask = np.where(jj <= ii, 0.0, NEGM).astype(np.float32)
    cmaskT = np.ascontiguousarray(cmask.T)
    cstv = f(np.concatenate([ident, cmask, cmaskT], axis=1))
    return {"aw": aw, "ab": ab, "pw": pw, "pb": pb, "gw": gw, "w13": w13b,
            "w2": w2b, "sw13": sw13b, "sw2": sw2b, "cst": cstv}


def kernel(L0_attn_hidden_states, L0_attn_c_attn_w, L0_attn_c_attn_b,
           L0_attn_c_proj_w, L0_attn_c_proj_b, L0_ffn_gate_w, L0_ffn_w13,
           L0_ffn_w2, L0_ffn_sw13, L0_ffn_sw2):
    c = CFG()
    x = np.asarray(L0_attn_hidden_states, dtype=np.float32)
    B = x.shape[0]
    shared = prep_shared(
        c, np.asarray(L0_attn_c_attn_w, np.float32),
        np.asarray(L0_attn_c_attn_b, np.float32),
        np.asarray(L0_attn_c_proj_w, np.float32),
        np.asarray(L0_attn_c_proj_b, np.float32),
        np.asarray(L0_ffn_gate_w, np.float32), np.asarray(L0_ffn_w13, np.float32),
        np.asarray(L0_ffn_w2, np.float32), np.asarray(L0_ffn_sw13, np.float32),
        np.asarray(L0_ffn_sw2, np.float32))
    nc = build_nc(c)
    in_maps = []
    for b in range(B):
        m = dict(shared)
        m["xT"] = np.ascontiguousarray(x[b].T)
        in_maps.append(m)
    res = run_bass_kernel_spmd(nc, in_maps, core_ids=list(range(B)))
    out = np.stack([np.asarray(r["outT"]).T for r in res.results], axis=0)
    return out.astype(np.float32)


# version 19
# speedup vs baseline: 1.0078x; 1.0078x over previous
import contextlib
import math

import numpy as np
import ml_dtypes

import concourse.bass as bass
import concourse.mybir as mybir
from concourse.bass_utils import run_bass_kernel_spmd

F32 = mybir.dt.float32
BF16 = mybir.dt.bfloat16
AF = mybir.ActivationFunctionType
ALU = mybir.AluOpType
AX = mybir.AxisListType.X

NEGM = -1.0e30
EPS = 1e-5
ENGINES = ("sync", "tensor", "scalar", "vector", "gpsimd")


class CFG:
    def __init__(self, D=2048, S=2048, NE=16, TOPK=6, I1=1408, IS=2816):
        self.D, self.S, self.NE, self.TOPK, self.I1, self.IS = D, S, NE, TOPK, I1, IS
        assert D % 128 == 0 and S % 1024 == 0 and I1 % 128 == 0 and IS % 256 == 0
        self.HC = D // 128
        self.NH = self.HC
        self.NT = S // 128
        self.HALF = S // 2
        self.NTH = self.HALF // 128
        self.J2 = I1 // 128
        self.JS2 = IS // 128
        assert IS == 2 * I1, "shared expert must split into two expert-shaped subs"
        assert self.JS2 % 2 == 0
        self.ALPHA = 1.0 / math.sqrt(128.0)


class Emit:

    def __init__(self, nc, stack):
        self.nc = nc
        self.stack = stack
        self.qs = {e: [] for e in ENGINES}
        self.sems = {}
        self.cnt = {}
        self.lastwait = {e: {} for e in ENGINES}
        for s in ("dW", "PE", "SC", "VE", "PO"):
            self.sem(s)

    def sem(self, s):
        if s not in self.sems:
            self.sems[s] = self.stack.enter_context(self.nc.semaphore(s))
            self.cnt[s] = 0
        return self.sems[s]

    strict = False
    strict_phases = None
    phase = "pre"

    def fence(self, eng):
        sem = {"vector": "VE", "scalar": "SC", "gpsimd": "PO", "tensor": "PE"}[eng]
        return self.op(eng, lambda e: e.drain(fusable=True), inc=sem, pub=False)

    def op(self, eng, fn, waits=(), inc=None, incv=1, pub=None):
        s_on = (Emit.strict if Emit.strict_phases is None
                else (Emit.phase in Emit.strict_phases))
        if s_on:
            waits = list(waits) + [(s, v) for s, v in self.cnt.items()]
        lw = self.lastwait[eng]
        eff = []
        for s, v in waits:
            if v and v > lw.get(s, 0):
                self.sem(s)
                eff.append((s, v))
                lw[s] = v
        after = None
        if inc is not None:
            self.sem(inc)
            self.cnt[inc] += incv
            after = self.cnt[inc]
        sems = self.sems
        if pub is None:
            pub = inc in ("SC", "VE", "PO")

        def closure(e, fn=fn, eff=tuple(eff), inc=inc, incv=incv, pub=pub):
            for s, v in eff:
                e.wait_ge(sems[s], v)
            instr = fn(e)
            if inc is not None:
                assert instr is not None, "op with inc must return an instruction"
                if pub:
                    e.drain(fusable=True).then_inc(sems[inc], incv)
                else:
                    instr.then_inc(sems[inc], incv)

        self.qs[eng].append(closure)
        return after

    def now(self, s):
        return self.cnt[s]

    def flush(self, block):
        for e in ENGINES:
            fns = self.qs[e]

            def runner(engine, fns=fns):
                for f in fns:
                    f(engine)

            getattr(block, e)(runner)


class WStream:

    def __init__(self, E, stack, nc, ns, cols, name):
        self.E, self.ns, self.cols = E, ns, cols
        self.wsf = [stack.enter_context(nc.sbuf_tensor(f"{name}f{i}", [128, cols], F32))
                    for i in range(ns)]
        self.wsb = [stack.enter_context(nc.sbuf_tensor(f"{name}b{i}", [128, cols], BF16))
                    for i in range(ns)]
        self.i = 0
        self.cast_done = {}
        self.use_done = {}
        self.sem_of = {"gpsimd": "PO", "scalar": "SC", "vector": "VE"}
        self.rr = ("gpsimd", "vector", "scalar")

    def load(self, dram_ap, ncols, cast=None):
        E = self.E
        i = self.i
        if cast is None:
            cast = self.rr[i % 3]
        self.i += 1
        slot = i % self.ns
        prev = i - self.ns
        waits = [self.cast_done[prev]] if prev >= 0 else []
        dsem = f"wsS{slot}"
        dv = E.op("sync",
                  lambda e, d=dram_ap, s=slot, w=ncols: e.dma_start(self.wsf[s][:, 0:w], d),
                  waits=waits, inc=dsem, incv=16)
        cwaits = [(dsem, dv)]
        if prev >= 0:
            assert prev in self.use_done, f"slab {prev} consumers not recorded"
            cwaits.append(("PE", self.use_done[prev]))
        sem = self.sem_of[cast]
        if cast == "scalar":
            fn = lambda e, s=slot, w=ncols: e.copy(self.wsb[s][:, 0:w],
                                                   self.wsf[s][:, 0:w])
        else:
            fn = lambda e, s=slot, w=ncols: e.tensor_copy(self.wsb[s][:, 0:w],
                                                          self.wsf[s][:, 0:w])
        cv = E.op(cast, fn, waits=cwaits, inc=sem)
        self.cast_done[i] = (sem, cv)
        return slot, i

    def ready(self, idx):
        return self.cast_done[idx]

    def done(self, idx, pe_val):
        self.use_done[idx] = pe_val


def build_nc(cfg=None, dbg=()):
    c = cfg or CFG()
    HC, NT, NE, AL = c.HC, c.NT, c.NE, c.ALPHA
    NH, S, D, HALF = c.NH, c.S, c.D, c.HALF
    J2, JS2 = c.J2, c.JS2
    NTC4 = S // 512
    nc = bass.Bass("TRN2", target_bir_lowering=False, detect_race_conditions=False)

    dt = nc.dram_tensor
    xT = dt("xT", [D, S], F32, kind="ExternalInput").ap()
    aw = dt("aw", [3 * NH, 128, HC, 128], F32, kind="ExternalInput").ap()
    ab = dt("ab", [128, 3 * NH], F32, kind="ExternalInput").ap()
    pw = dt("pw", [HC, 128, HC, 128], F32, kind="ExternalInput").ap()
    pb = dt("pb", [128, HC], F32, kind="ExternalInput").ap()
    gw = dt("gw", [HC, 128, NE], F32, kind="ExternalInput").ap()
    w13 = dt("w13", [NE, 2 * J2, 128, HC, 128], F32, kind="ExternalInput").ap()
    w2 = dt("w2", [NE, HC, 128, J2, 128], F32, kind="ExternalInput").ap()
    sw13 = dt("sw13", [2 * JS2, 128, HC, 128], F32, kind="ExternalInput").ap()
    sw2 = dt("sw2", [HC, 128, JS2, 128], F32, kind="ExternalInput").ap()
    cst = dt("cst", [128, 384], F32, kind="ExternalInput").ap()
    outT = dt("outT", [D, S], F32, kind="ExternalOutput").ap()
    x1T = dt("x1T", [D, S], F32, kind="Internal").ap()
    aTs = dt("aTs", [D, S], F32, kind="Internal").ap()
    ln2s = dt("ln2s", [D, S], BF16, kind="Internal").ap()
    combTd = dt("combTd", [NE, S], F32, kind="Internal").ap()
    dbg_t = {}
    for name, shape, dtp in [
        ("dbg_ln1", [D, S], BF16), ("dbg_attn", [128, NH * S], BF16),
        ("dbg_x1", [D, S], F32), ("dbg_ln2", [D, S], BF16),
        ("dbg_comb", [NE, S], F32), ("dbg_acc", [D, HALF], F32),
        ("dbg_a", [D, S], F32), ("dbg_acc1", [D, HALF], F32),
        ("dbg_lnr", [128, 16384], BF16),
    ]:
        if name in dbg:
            dbg_t[name] = dt(name, shape, dtp, kind="ExternalOutput").ap()

    with contextlib.ExitStack() as st:
        en = st.enter_context
        E = Emit(nc, st)
        op, now = E.op, E.now

        ident = en(nc.sbuf_tensor("ident", [128, 128], F32))
        cmask = en(nc.sbuf_tensor("cmaskS", [128, 128], F32))
        cmaskT = en(nc.sbuf_tensor("cmaskTS", [128, 128], F32))
        ones1b = en(nc.sbuf_tensor("ones1b", [1, 128], BF16))
        ones1f = en(nc.sbuf_tensor("ones1f", [1, 128], F32))
        onescb = en(nc.sbuf_tensor("onescb", [128, 1], BF16))
        onescf = en(nc.sbuf_tensor("onescf", [128, 1], F32))
        epst = en(nc.sbuf_tensor("epst", [128, 1], F32))
        abs_ = en(nc.sbuf_tensor("abs_", [128, 3 * NH], F32))
        pbs = en(nc.sbuf_tensor("pbs", [128, HC], F32))
        gws = en(nc.sbuf_tensor("gws", [128, HC * NE], F32))
        gsum = en(nc.sbuf_tensor("gsum", [1, NE], F32))
        lnA = en(nc.sbuf_tensor("lnA", [128, HC * HALF], BF16))
        sstat = en(nc.sbuf_tensor("sstat", [128, 4 * 512], BF16))
        mrow = en(nc.sbuf_tensor("mrow", [1, 512], F32))
        vrow = en(nc.sbuf_tensor("vrow", [1, 512], F32))
        trow = en(nc.sbuf_tensor("trow", [1, 512], F32))
        rrow = en(nc.sbuf_tensor("rrow", [1, 512], F32))
        fsc = en(nc.sbuf_tensor("fsc", [128, 2 * 512], F32))
        SLABC = max(HC * 128, J2 * 128)
        ws = WStream(E, st, nc, 3, SLABC, "ws")
        PS = en(nc.psum_tensor("PS", [128, 4096], F32))

        def bk(i, lo=0, hi=512):
            return PS[:, i * 512 + lo:i * 512 + hi]

        def bkr(i, lo, hi, p0=0, p1=1):
            return PS[p0:p1, i * 512 + lo:i * 512 + hi]

        for dst, src in [(ident[:], cst[:, 0:128]), (cmask[:], cst[:, 128:256]),
                         (cmaskT[:], cst[:, 256:384]), (abs_[:], ab), (pbs[:], pb)]:
            op("sync", lambda e, d=dst, s=src: e.dma_start(d, s), inc="dW", incv=16)
        d_gw = op("sync", lambda e: e.dma_start(
            gws[:].rearrange("p (h e) -> p h e", h=HC),
            gw.rearrange("h p e -> p h e")), inc="dW", incv=16)
        op("vector", lambda e: e.memset(ones1b[:], 1.0), inc="VE", pub=False)
        op("vector", lambda e: e.memset(ones1f[:], 1.0), inc="VE", pub=False)
        op("vector", lambda e: e.memset(onescb[:], 1.0), inc="VE", pub=False)
        op("vector", lambda e: e.memset(onescf[:], 1.0), inc="VE")
        ve_pre = op("vector", lambda e: e.memset(epst[:], EPS), inc="VE")

        for h in range(HC):
            op("tensor", lambda e, h=h: e.matmul(
                bkr(2, 0, NE), onescf[:], gws[:, h * NE:(h + 1) * NE],
                start=(h == 0), stop=(h == HC - 1)),
                waits=[("dW", d_gw), ("VE", ve_pre)],
                inc="PE" if h == HC - 1 else None)
        sc_gsum = op("scalar", lambda e: e.copy(gsum[:], bkr(2, 0, NE)),
                     waits=[("PE", now("PE"))], inc="SC")

        def emit_ln_stats(src_fn, tw, bsum, bsq, bm, br, waits):
            pe_val = {}
            for h in range(HC):
                r = h % 2
                w = list(waits)
                if h >= 2:
                    w = w + [("PE", pe_val[h - 2])]
                src_ap = src_fn(h)
                v1 = op("vector", lambda e, a=src_ap, r=r: e.tensor_copy(
                    sstat[:, r * 1024:r * 1024 + tw], a), waits=w, inc="VE")
                s1 = op("scalar", lambda e, a=src_ap, r=r: e.activation(
                    sstat[:, r * 1024 + 512:r * 1024 + 512 + tw], a, AF.Square),
                    waits=w, inc="SC")
                op("tensor", lambda e, h=h, r=r: e.matmul(
                    bkr(bsum, 0, tw), onescb[:], sstat[:, r * 1024:r * 1024 + tw],
                    start=(h == 0), stop=(h == HC - 1)),
                    waits=[("VE", v1)], inc="PE")
                op("tensor", lambda e, h=h, r=r: e.matmul(
                    bkr(bsq, 0, tw), onescb[:],
                    sstat[:, r * 1024 + 512:r * 1024 + 512 + tw],
                    start=(h == 0), stop=(h == HC - 1)),
                    waits=[("SC", s1)], inc="PE")
                pe_val[h] = now("PE")
            pe_sums = now("PE")
            op("vector", lambda e: e.tensor_scalar(
                mrow[:, 0:tw], bkr(bsum, 0, tw), 1.0 / D, None, ALU.mult),
                waits=[("PE", pe_sums)], inc="VE")
            op("vector", lambda e: e.tensor_scalar(
                vrow[:, 0:tw], bkr(bsq, 0, tw), 1.0 / D, None, ALU.mult),
                inc="VE")
            op("vector", lambda e: e.tensor_mul(trow[:, 0:tw], mrow[:, 0:tw],
                                                mrow[:, 0:tw]), inc="VE", pub=False)
            ve_m = op("vector", lambda e: e.tensor_sub(vrow[:, 0:tw], vrow[:, 0:tw],
                                                       trow[:, 0:tw]), inc="VE")
            op("scalar", lambda e: e.activation(rrow[:, 0:tw], vrow[:, 0:tw], AF.Ln,
                                                bias=epst[0:1, :]),
               waits=[("VE", ve_m)], inc="SC")
            sc_r = op("scalar", lambda e: e.activation(rrow[:, 0:tw], rrow[:, 0:tw],
                                                       AF.Exp, scale=-0.5), inc="SC")
            op("tensor", lambda e: e.matmul(bk(bm, 0, tw), ones1f[:], mrow[:, 0:tw],
                                            start=True, stop=True),
               waits=[("VE", ve_m)], inc="PE")
            pe_bc = op("tensor", lambda e: e.matmul(bk(br, 0, tw), ones1f[:],
                                                    rrow[:, 0:tw], start=True, stop=True),
                       waits=[("SC", sc_r)], inc="PE")
            return {"pe_bcast": pe_bc, "sc_r": sc_r, "ve_m": ve_m, "pe_sums": pe_sums}

        Emit.phase = "p0"
        ln1_stack = contextlib.ExitStack()
        ln1b = ln1_stack.enter_context(
            nc.sbuf_tensor("ln1b", [128, HC * HALF], BF16, side="right"))

        def ln1_ap(hcx, t0, twid):
            if t0 < HALF:
                assert t0 + twid <= HALF
                return lnA[:, hcx * HALF + t0:hcx * HALF + t0 + twid]
            return ln1b[:, hcx * HALF + (t0 - HALF):hcx * HALF + (t0 - HALF) + twid]

        with nc.sbuf_tensor("xc0", [128, HC * 512], F32) as xc0:
            last_ve = 0
            for tc in range(NTC4):
                wd = [("VE", ve_pre)]
                if tc > 0:
                    wd += [("VE", now("VE")), ("SC", now("SC"))]
                dv = op("gpsimd", lambda e, tc=tc: e.dma_start(
                    xc0[:].rearrange("p (h t) -> p h t", h=HC),
                    xT[:, tc * 512:(tc + 1) * 512].rearrange("(h p) t -> p h t", p=128)),
                    waits=wd, inc="sXC0", incv=16)
                d0 = emit_ln_stats(lambda h: xc0[:, h * 512:(h + 1) * 512], 512,
                                   2, 3, 4, 5, [("sXC0", dv), ("SC", sc_gsum)])
                for h in range(HC):
                    r = h % 2
                    op("vector", lambda e, h=h, r=r: e.tensor_sub(
                        fsc[:, r * 512:(r + 1) * 512], xc0[:, h * 512:(h + 1) * 512],
                        bk(4, 0, 512)), waits=[("PE", d0["pe_bcast"])], inc="VE",
                        pub=False)
                    last_ve = op("vector", lambda e, h=h, r=r, tc=tc: e.tensor_mul(
                        ln1_ap(h, tc * 512, 512), fsc[:, r * 512:(r + 1) * 512],
                        bk(5, 0, 512)), inc="VE", pub=False)
            ve_ln1_done = E.fence("vector")
        if "dbg_ln1" in dbg_t:
            for hcx in range(HC):
                op("gpsimd", lambda e, h=hcx: e.dma_start(
                    dbg_t["dbg_ln1"][h * 128:(h + 1) * 128, 0:HALF],
                    lnA[:, h * HALF:(h + 1) * HALF]),
                    waits=[("VE", ve_ln1_done)], inc="sDBG", incv=16)
                op("gpsimd", lambda e, h=hcx: e.dma_start(
                    dbg_t["dbg_ln1"][h * 128:(h + 1) * 128, HALF:S],
                    ln1b[:, h * HALF:(h + 1) * HALF]), inc="sDBG", incv=16)

        Emit.phase = "p1"
        p1 = contextlib.ExitStack()
        e1 = p1.enter_context
        out_all = e1(nc.sbuf_tensor("out_all", [128, NH * S], BF16))
        p1s = contextlib.ExitStack()
        e1s = p1s.enter_context
        qT = e1s(nc.sbuf_tensor("qT", [128, S], BF16))
        kT = e1s(nc.sbuf_tensor("kT", [128, S], BF16))
        vS = e1s(nc.sbuf_tensor("vS", [128, S], BF16))
        frows = e1s(nc.sbuf_tensor("frows", [1, S], BF16))
        frows2 = e1s(nc.sbuf_tensor("frows2", [1, 512], BF16))
        PTs = [e1s(nc.sbuf_tensor(f"PT{i}", [128, 512], BF16)) for i in range(2)]
        cols = e1s(nc.sbuf_tensor("cols", [128, 8], F32))

        sc_head_tail = 0
        ve_bt_tail = 0
        pt_pe = [0, 0]
        pe_qkv_last = 0
        for h in range(NH):
            Emit.phase = "p1qkv"
            sq, isq = ws.load(aw[h].rearrange("p h j -> p (h j)"), HC * 128)
            sk, isk = ws.load(aw[NH + h].rearrange("p h j -> p (h j)"), HC * 128)
            sv, isv = ws.load(aw[2 * NH + h].rearrange("p h j -> p (h j)"), HC * 128)
            sc_evac = {}
            for which, slab, islab, dstT, bias_i in (
                    ("q", sq, isq, qT, h), ("k", sk, isk, kT, NH + h)):
                for tc in range(NTC4):
                    bq = tc % 2
                    w = [ws.ready(islab), ("VE", ve_ln1_done), ("SC", sc_head_tail)]
                    if (which, tc - 2) in sc_evac:
                        w.append(("SC", sc_evac[(which, tc - 2)]))
                    if which == "k":
                        prevq = (S // 512) - 2 + (tc % 2)
                        if ("q", prevq) in sc_evac:
                            w.append(("SC", sc_evac[("q", prevq)]))
                    for hx in range(HC):
                        op("tensor", lambda e, s=slab, hx=hx, tc=tc, bq=bq: e.matmul(
                            bk(bq), ws.wsb[s][:, hx * 128:(hx + 1) * 128],
                            ln1_ap(hx, tc * 512, 512),
                            start=(hx == 0), stop=(hx == HC - 1)),
                            waits=(w if hx == 0 else []),
                            inc=("PE" if hx == HC - 1 else None))
                    pe_g = now("PE")
                    sc_evac[(which, tc)] = op(
                        "scalar", lambda e, d=dstT, tc=tc, bq=bq, bi=bias_i: e.activation(
                            d[:, tc * 512:(tc + 1) * 512], bk(bq), AF.Identity,
                            bias=abs_[:, bi:bi + 1]),
                        waits=[("PE", pe_g)], inc="SC")
            ws.done(isq, now("PE"))
            ws.done(isk, now("PE"))
            for tt in range(NT):
                bv = 2 + (tt // 4) % 2
                reg = tt % 4
                w = [ws.ready(isv), ("VE", ve_ln1_done), ("SC", sc_head_tail)]
                if ("v", tt // 4 - 2) in sc_evac:
                    w.append(("SC", sc_evac[("v", tt // 4 - 2)]))
                for hx in range(HC):
                    op("tensor", lambda e, hx=hx, tt=tt, bv=bv, reg=reg, s=sv: e.matmul(
                        bk(bv, reg * 128, reg * 128 + 128),
                        ln1_ap(hx, tt * 128, 128),
                        ws.wsb[s][:, hx * 128:(hx + 1) * 128],
                        start=(hx == 0), stop=(hx == HC - 1)),
                        waits=(w if hx == 0 else []),
                        inc=("PE" if hx == HC - 1 else None))
                if reg == 3:
                    pe_g = now("PE")
                    sc_evac[("v", tt // 4)] = op(
                        "scalar", lambda e, tt=tt, bv=bv: e.copy(
                            vS[:, (tt - 3) * 128:(tt + 1) * 128], bk(bv)),
                        waits=[("PE", pe_g)], inc="SC")
            ws.done(isv, now("PE"))
            sc_qkv_done = sc_evac[("v", NT // 4 - 1)]
            pe_qkv_last = now("PE")

            Emit.phase = "p1a"
            sc_frows = {}
            ve_max_v = {}
            pend = None
            for qi in range(NT):
                buf = qi % 2
                base = buf * 2048
                kw = (qi + 1) * 128
                r = qi % 4
                w = [("SC", sc_qkv_done), ("dW", d_gw), ("VE", ve_bt_tail)]
                if qi - 2 in ve_max_v:
                    w.append(("VE", ve_max_v[qi - 2]))
                offs = list(range(0, kw, 512))[::-1]
                first = True
                for off in offs:
                    wd = min(512, kw - off)
                    wc = list(w) if first else []
                    if off == 0:
                        wc = wc + [("SC", sc_frows.get(qi - 2, 0))]
                    op("tensor", lambda e, qi=qi, off=off, wd=wd, base=base: e.matmul(
                        PS[:, base + off:base + off + wd],
                        qT[:, qi * 128:(qi + 1) * 128], kT[:, off:off + wd],
                        start=True, stop=True),
                        waits=wc,
                        inc=("PE" if off == 0 else None))
                    first = False
                pe_row = now("PE")
                op("vector", lambda e, qi=qi, base=base, kw=kw: e.tensor_tensor(
                    PS[:, base + qi * 128:base + kw], PS[:, base + qi * 128:base + kw],
                    cmask[:], ALU.add), waits=[("PE", pe_row)], inc="VE", pub=False)
                ve_mx = op("vector", lambda e, base=base, kw=kw, r=r: e.reduce_max(
                    cols[:, r:r + 1], PS[:, base:base + kw], axis=AX, negate=True),
                    inc="VE")
                ve_max_v[qi] = ve_mx
                if pend is not None:
                    pqi, pr, pvf = pend
                    tbase = (pqi % 2) * 2048
                    pe_tr = op("tensor", lambda e, tb=tbase, r=pr: e.transpose(
                        PS[0:1, tb:tb + 128], cols[:, r:r + 1], ident[:]),
                        waits=[("VE", pvf), ("SC", sc_frows.get(pqi - 2, 0))],
                        inc="PE")
                    sc_frows[pqi] = op("scalar", lambda e, tb=tbase, qi=pqi: e.copy(
                        frows[0:1, qi * 128:(qi + 1) * 128], PS[0:1, tb:tb + 128]),
                        waits=[("PE", pe_tr)], inc="SC")
                pend = (qi, r, ve_mx)
            pqi, pr, pvf = pend
            tbase = (pqi % 2) * 2048
            pe_tr = op("tensor", lambda e, tb=tbase, r=pr: e.transpose(
                PS[0:1, tb:tb + 128], cols[:, r:r + 1], ident[:]),
                waits=[("VE", pvf)], inc="PE")
            sc_frows[pqi] = op("scalar", lambda e, tb=tbase, qi=pqi: e.copy(
                frows[0:1, qi * 128:(qi + 1) * 128], PS[0:1, tb:tb + 128]),
                waits=[("PE", pe_tr)], inc="SC")
            sc_pa_done = sc_frows[NT - 1]

            Emit.phase = "p1b"
            gslot = 0
            stp_sc = [0, 0]
            sc_outev = 0
            ve_outev = 0
            sc_rb = 0
            pe_pv = 0
            for qi in range(NT):
                nk = qi + 1
                for g0 in range(0, nk, 4):
                    gwd = min(4, nk - g0)
                    slot = gslot % 2
                    gslot += 1
                    base = (4 + slot) * 512
                    w = [("SC", sc_pa_done), ("SC", stp_sc[slot]), ("SC", sc_rb)]
                    first = True
                    for j in range(gwd):
                        kc = g0 + j
                        op("tensor", lambda e, j=j, kc=kc, qi=qi, base=base: e.matmul(
                            PS[:, base + j * 128:base + (j + 1) * 128],
                            kT[:, kc * 128:(kc + 1) * 128],
                            qT[:, qi * 128:(qi + 1) * 128],
                            start=True, stop=False),
                            waits=(w if first else []), inc=None)
                        first = False
                        op("tensor", lambda e, j=j, qi=qi, base=base: e.matmul(
                            PS[:, base + j * 128:base + (j + 1) * 128],
                            ones1b[:], frows[0:1, qi * 128:(qi + 1) * 128],
                            start=False, stop=True),
                            inc=("PE" if j == gwd - 1 else None))
                    pe_grp = now("PE")
                    wexp = [("PE", pe_grp), ("PE", pt_pe[slot])]
                    if g0 <= qi < g0 + gwd:
                        jd = qi - g0
                        ve_d = op("vector", lambda e, jd=jd, base=base: e.tensor_tensor(
                            PS[:, base + jd * 128:base + (jd + 1) * 128],
                            PS[:, base + jd * 128:base + (jd + 1) * 128],
                            cmaskT[:], ALU.add), waits=[("PE", pe_grp)], inc="VE")
                        wexp.append(("VE", ve_d))
                    sc_e = op("scalar", lambda e, slot=slot, base=base, gwd=gwd: e.activation(
                        PTs[slot][:, 0:gwd * 128], PS[:, base:base + gwd * 128],
                        AF.Exp, scale=AL), waits=wexp, inc="SC")
                    stp_sc[slot] = sc_e
                    wpv = [("SC", sc_e), ("SC", sc_outev), ("VE", ve_outev)]
                    for j in range(gwd):
                        kc = g0 + j
                        op("tensor", lambda e, j=j, kc=kc, qi=qi, slot=slot: e.matmul(
                            bk(6, (qi % 4) * 128, (qi % 4) * 128 + 128),
                            vS[:, kc * 128:(kc + 1) * 128],
                            PTs[slot][:, j * 128:(j + 1) * 128],
                            start=(kc == 0), stop=(kc == qi)),
                            waits=(wpv if j == 0 else []), inc=None)
                        op("tensor", lambda e, j=j, kc=kc, qi=qi, slot=slot: e.matmul(
                            PS[0:1, 3584 + (qi % 4) * 128:3584 + (qi % 4) * 128 + 128],
                            onescb[:], PTs[slot][:, j * 128:(j + 1) * 128],
                            start=(kc == 0), stop=(kc == qi)),
                            inc=("PE" if kc == qi else None))
                    pt_pe[slot] = now("PE")
                pe_pv = now("PE")
                if qi % 4 == 3:
                    op("vector", lambda e: e.reciprocal(
                        rrow[:, 0:512], PS[0:1, 3584:3584 + 512]),
                        waits=[("PE", pe_pv)], inc="VE")
                    ve_rc = op("vector", lambda e: e.tensor_copy(
                        frows2[0:1, 0:512], rrow[:, 0:512]), inc="VE")
                    pe_rb = op("tensor", lambda e: e.matmul(
                        PS[:, 3584:3584 + 512], ones1b[:], frows2[0:1, 0:512],
                        start=True, stop=True), waits=[("VE", ve_rc)], inc="PE")
                    sc_rb = op("scalar", lambda e: e.copy(
                        fsc[:, 512:1024], PS[:, 3584:3584 + 512]),
                        waits=[("PE", pe_rb)], inc="SC")
                    ve_outev = op("vector", lambda e: e.tensor_mul(
                        fsc[:, 0:512], bk(6), fsc[:, 512:1024]),
                        waits=[("SC", sc_rb), ("SC", sc_outev)], inc="VE")
                    sc_outev = op("scalar", lambda e, h=h, qi=qi: e.activation(
                        out_all[:, h * S + (qi - 3) * 128:h * S + (qi + 1) * 128],
                        fsc[:, 0:512], AF.Identity,
                        bias=abs_[:, 2 * NH + h:2 * NH + h + 1]),
                        waits=[("VE", ve_outev)], inc="SC")
            sc_head_tail = sc_outev
            ve_bt_tail = ve_outev
        sc_attn_done = sc_head_tail
        pe_ln1_free = pe_qkv_last
        if "dbg_attn" in dbg_t:
            op("gpsimd", lambda e: e.dma_start(dbg_t["dbg_attn"], out_all[:]),
               waits=[("SC", sc_attn_done)], inc="sDBG", incv=16)

        Emit.phase = "p2a"
        p1s.close()
        asb = e1(nc.sbuf_tensor("asb", [128, 2 * 512], F32))
        asb_hist = {}
        for dac in range(HC):
            sp, isp = ws.load(pw[dac].rearrange("p h j -> p (h j)"), HC * 128)
            for tcw in range(NTC4):
                gi = dac * NTC4 + tcw
                bq = tcw % 2
                w = [ws.ready(isp), ("SC", sc_attn_done)]
                if gi - 2 in asb_hist:
                    w.append(("SC", asb_hist[gi - 2][0]))
                for dcx in range(HC):
                    op("tensor", lambda e, s=sp, dcx=dcx, tcw=tcw, bq=bq: e.matmul(
                        bk(bq), ws.wsb[s][:, dcx * 128:(dcx + 1) * 128],
                        out_all[:, dcx * S + tcw * 512:dcx * S + (tcw + 1) * 512],
                        start=(dcx == 0), stop=(dcx == HC - 1)),
                        waits=(w if dcx == 0 else []),
                        inc=("PE" if dcx == HC - 1 else None))
                pe_g = now("PE")
                ring = gi % 2
                wsc = [("PE", pe_g)]
                if gi - 2 in asb_hist:
                    wsc.append((f"sASB{ring}", asb_hist[gi - 2][1]))
                sc_a = op("scalar", lambda e, ring=ring, bq=bq, dac=dac: e.activation(
                    asb[:, ring * 512:(ring + 1) * 512], bk(bq), AF.Identity,
                    bias=pbs[:, dac:dac + 1]), waits=wsc, inc="SC")
                dxv = op("gpsimd", lambda e, ring=ring, dac=dac, tcw=tcw: e.dma_start(
                    aTs[dac * 128:(dac + 1) * 128, tcw * 512:(tcw + 1) * 512],
                    asb[:, ring * 512:(ring + 1) * 512]),
                    waits=[("SC", sc_a)], inc=f"sASB{ring}", incv=16)
                asb_hist[gi] = (sc_a, dxv)
            ws.done(isp, now("PE"))
        p2a_done = [("sASB0", E.cnt.get("sASB0", 0)), ("sASB1", E.cnt.get("sASB1", 0))]
        if "dbg_a" in dbg_t:
            op("gpsimd", lambda e: e.dma_start(dbg_t["dbg_a"], aTs),
               waits=p2a_done, inc="sDBG", incv=16)
        p1.close()
        ln1_stack.close()

        Emit.phase = "p2b"
        ve_lnA_w = 0
        dx_x1_done = 0
        dx_comb_done = 0
        p2 = contextlib.ExitStack()
        e2 = p2.enter_context
        ac = e2(nc.sbuf_tensor("ac", [128, HC * 128], F32))
        xc2 = e2(nc.sbuf_tensor("xc2", [128, HC * 128], F32))
        lnt = e2(nc.sbuf_tensor("lnt", [128, HC * 128], BF16))
        gl = e2(nc.sbuf_tensor("gl", [128, NE], F32))
        gp = e2(nc.sbuf_tensor("gp", [128, NE], F32))
        gwk = e2(nc.sbuf_tensor("gwk", [128, NE], F32))
        gm = e2(nc.sbuf_tensor("gm", [128, 4], F32))
        r2c = e2(nc.sbuf_tensor("r2c", [128, 1], F32))
        ctile = e2(nc.sbuf_tensor("ctile", [NE, 128], F32))
        n2row = e2(nc.sbuf_tensor("n2row", [1, 128], F32))

        prev_ct_sc = 0
        prev_pe_pg = 0
        dx_lnt_prev = 0
        dx_x1 = 0
        for tc in range(NT):
            wdma = list(p2a_done)
            if tc > 0:
                wdma += [("VE", now("VE")), ("SC", now("SC")), ("PE", now("PE"))]
            dva = op("gpsimd", lambda e, tc=tc: e.dma_start(
                ac[:].rearrange("p (h t) -> p h t", h=HC),
                aTs[:, tc * 128:(tc + 1) * 128].rearrange("(h p) t -> p h t", p=128)),
                waits=wdma, inc="sAC", incv=16)
            wdmx = [("sX1", dx_x1)] if tc > 0 else []
            dvx = op("gpsimd", lambda e, tc=tc: e.dma_start(
                xc2[:].rearrange("p (h t) -> p h t", h=HC),
                xT[:, tc * 128:(tc + 1) * 128].rearrange("(h p) t -> p h t", p=128)),
                waits=wdmx, inc="sXC2", incv=16)
            d1 = emit_ln_stats(lambda hx: ac[:, hx * 128:(hx + 1) * 128], 128,
                               2, 3, 4, 5, [("sAC", dva)])
            last_ve = 0
            for hx in range(HC):
                r = hx % 2
                w = [("PE", d1["pe_bcast"])] + ([("sXC2", dvx)] if hx == 0 else [])
                op("vector", lambda e, hx=hx, r=r: e.tensor_sub(
                    fsc[:, r * 512:r * 512 + 128], ac[:, hx * 128:(hx + 1) * 128],
                    bk(4, 0, 128)), waits=w, inc="VE", pub=False)
                op("vector", lambda e, r=r: e.tensor_mul(
                    fsc[:, r * 512:r * 512 + 128], fsc[:, r * 512:r * 512 + 128],
                    bk(5, 0, 128)), inc="VE", pub=False)
                last_ve = op("vector", lambda e, hx=hx, r=r: e.tensor_add(
                    xc2[:, hx * 128:(hx + 1) * 128], xc2[:, hx * 128:(hx + 1) * 128],
                    fsc[:, r * 512:r * 512 + 128]), inc="VE",
                    pub=(hx == HC - 1))
            dx_x1 = op("gpsimd", lambda e, tc=tc: e.dma_start(
                x1T[:, tc * 128:(tc + 1) * 128].rearrange("(h p) t -> p h t", p=128),
                xc2[:].rearrange("p (h t) -> p h t", h=HC)),
                waits=[("VE", last_ve)], inc="sX1", incv=16)
            dx_x1_done = dx_x1
            d2 = emit_ln_stats(lambda hx: xc2[:, hx * 128:(hx + 1) * 128], 128,
                               2, 3, 6, 7, [("VE", last_ve)])
            last_ve2 = 0
            half1 = tc * 128 >= HALF
            for hx in range(HC):
                r = hx % 2
                w = [("PE", d2["pe_bcast"])]
                if hx == 0:
                    if half1 and dx_lnt_prev:
                        w.append(("sLNT", dx_lnt_prev))
                    if not half1 and tc == 0:
                        w.append(("PE", pe_ln1_free))
                op("vector", lambda e, hx=hx, r=r: e.tensor_sub(
                    fsc[:, r * 512:r * 512 + 128], xc2[:, hx * 128:(hx + 1) * 128],
                    bk(6, 0, 128)), waits=w, inc="VE", pub=False)
                dst = (lnt[:, hx * 128:(hx + 1) * 128] if half1 else
                       lnA[:, hx * HALF + tc * 128:hx * HALF + (tc + 1) * 128])
                last_ve2 = op("vector", lambda e, d=dst, r=r: e.tensor_mul(
                    d, fsc[:, r * 512:r * 512 + 128], bk(7, 0, 128)), inc="VE",
                    pub=(hx == HC - 1))
            if half1:
                dx_lnt_prev = op("gpsimd", lambda e, tc=tc: e.dma_start(
                    ln2s[:, tc * 128:(tc + 1) * 128].rearrange("(h p) t -> p h t", p=128),
                    lnt[:].rearrange("p (h t) -> p h t", h=HC)),
                    waits=[("VE", last_ve2)], inc="sLNT", incv=16)
            ve_lnA_w = max(ve_lnA_w, last_ve2)
            for hx in range(HC):
                w = [("VE", last_ve), ("VE", d2["ve_m"])] if hx == 0 else []
                op("tensor", lambda e, hx=hx: e.matmul(
                    bk(3, 256, 256 + NE), xc2[:, hx * 128:(hx + 1) * 128],
                    gws[:, hx * NE:(hx + 1) * NE], start=(hx == 0), stop=False),
                    waits=w, inc=None)
            ve_n2 = op("vector", lambda e: e.tensor_scalar(
                n2row[:], mrow[:, 0:128], -1.0, None, ALU.mult),
                waits=[("VE", d2["ve_m"]), ("PE", prev_pe_pg)], inc="VE")
            pe_pg = op("tensor", lambda e: e.matmul(
                bk(3, 256, 256 + NE), n2row[:], gsum[:], start=False, stop=True),
                waits=[("VE", ve_n2), ("SC", sc_gsum)], inc="PE")
            prev_pe_pg = pe_pg
            pe_r2 = op("tensor", lambda e: e.transpose(
                bk(1, 0, 1), rrow[0:1, 0:128], ident[0:1, 0:1]),
                waits=[("SC", d2["sc_r"]), ("SC", prev_ct_sc)], inc="PE")
            sc_r2 = op("scalar", lambda e: e.copy(r2c[:], bk(1, 0, 1)),
                       waits=[("PE", pe_r2)], inc="SC")
            op("vector", lambda e: e.tensor_scalar(
                gl[:], bk(3, 256, 256 + NE), r2c[:], None, ALU.mult),
                waits=[("PE", pe_pg), ("SC", sc_r2)], inc="VE", pub=False)
            ve_gm = op("vector", lambda e: e.reduce_max(gm[:, 0:1], gl[:], axis=AX,
                                                        negate=True), inc="VE")
            sc_gp = op("scalar", lambda e: e.activation(
                gp[:], gl[:], AF.Exp, bias=gm[:, 0:1], accum_out=gm[:, 1:2]),
                waits=[("VE", ve_gm)], inc="SC")
            op("vector", lambda e: e.reciprocal(gm[:, 2:3], gm[:, 1:2]),
               waits=[("SC", sc_gp)], inc="VE")
            op("vector", lambda e: e.tensor_scalar(
                gp[:], gp[:], gm[:, 2:3], None, ALU.mult), inc="VE")
            op("vector", lambda e: e.tensor_copy(gwk[:], gp[:]), inc="VE")
            for _ in range(c.TOPK):
                op("vector", lambda e: e.reduce_max(gm[:, 3:4], gwk[:], axis=AX),
                   inc="VE")
                op("vector", lambda e: e.tensor_single_scalar(
                    gl[:], gwk[:], gm[:, 3:4], ALU.is_ge), inc="VE")
                op("vector", lambda e: e.scalar_tensor_tensor(
                    gwk[:], gl[:], -2.0, gwk[:], ALU.mult, ALU.add), inc="VE")
            op("vector", lambda e: e.tensor_sub(gl[:], gp[:], gwk[:]), inc="VE")
            op("vector", lambda e: e.scalar_tensor_tensor(
                gl[:], gl[:], 0.5, gp[:], ALU.mult, ALU.mult), inc="VE")
            op("vector", lambda e: e.reduce_sum(gm[:, 3:4], gl[:], axis=AX), inc="VE")
            op("vector", lambda e: e.tensor_scalar(
                gm[:, 3:4], gm[:, 3:4], 1e-20, None, ALU.add), inc="VE")
            op("vector", lambda e: e.reciprocal(gm[:, 2:3], gm[:, 3:4]), inc="VE")
            ve_cb = op("vector", lambda e: e.tensor_scalar(
                gl[:], gl[:], gm[:, 2:3], None, ALU.mult), inc="VE")
            pe_ct = op("tensor", lambda e: e.transpose(
                bkr(1, 128, 256, 0, NE), gl[:, 0:NE], ident[:]),
                waits=[("VE", ve_cb)], inc="PE")
            sc_ct = op("scalar", lambda e: e.copy(ctile[:], bkr(1, 128, 256, 0, NE)),
                       waits=[("PE", pe_ct), ("sCT", dx_comb_done)], inc="SC")
            dx_ct = op("gpsimd", lambda e, tc=tc: e.dma_start(
                combTd[:, tc * 128:(tc + 1) * 128], ctile[:]),
                waits=[("SC", sc_ct)], inc="sCT", incv=16)
            prev_ct_sc = sc_ct
            dx_comb_done = dx_ct
            prev_ac_readers = [("VE", last_ve), ("SC", now("SC"))]
            prev_xc2_readers = [("VE", last_ve2), ("PE", pe_pg)]
        ve_p2_done = now("VE")
        sc_p2_done = now("SC")
        p2.close()
        if "dbg_x1" in dbg_t:
            op("gpsimd", lambda e: e.dma_start(dbg_t["dbg_x1"], x1T),
               waits=[("sX1", dx_x1_done)], inc="sDBG", incv=16)
        if "dbg_ln2" in dbg_t:
            for hcx in range(HC):
                op("gpsimd", lambda e, hx=hcx: e.dma_start(
                    dbg_t["dbg_ln2"][hx * 128:(hx + 1) * 128, 0:HALF],
                    lnA[:, hx * HALF:(hx + 1) * HALF]),
                    waits=[("VE", ve_p2_done)], inc="sDBG", incv=16)
            op("gpsimd", lambda e: e.dma_start(
                dbg_t["dbg_ln2"][:, HALF:S], ln2s[:, HALF:S]),
                waits=[("sLNT", dx_lnt_prev)], inc="sDBG", incv=16)
        if "dbg_comb" in dbg_t:
            op("gpsimd", lambda e: e.dma_start(dbg_t["dbg_comb"], combTd),
               waits=[("sCT", dx_comb_done)], inc="sDBG", incv=16)

        Emit.phase = "p3"
        acc = en(nc.sbuf_tensor("acc", [128, HC * HALF], F32, side="right"))
        p3 = contextlib.ExitStack()
        e3 = p3.enter_context

        def w13_slab(ei, jc):
            return w13[ei, jc].rearrange("p h j -> p (h j)")

        def sw13_slab(jc):
            return sw13[jc].rearrange("p h j -> p (h j)")

        bank_sc = {}
        bank_ve = {}
        sgb_ve = {}
        usb_ve = {}
        crow_pe = 0
        cbs_ve_free = [0, 0]
        acc_read_ve = 0
        lnA_pe_last = 0
        ve_out_done = 0
        dx_out = 0
        lnA_ready = ("VE", ve_lnA_w)

        for half in range(2):
            t0 = half * HALF
            half_entry = [(s, E.cnt[s]) for s in list(E.cnt)]
            p3h = contextlib.ExitStack()
            e3h = p3h.enter_context
            act0 = e3h(nc.sbuf_tensor(f"act0_{half}", [128, J2 * HALF], BF16))
            act1 = e3h(nc.sbuf_tensor(f"act1_{half}", [128, J2 * HALF], BF16))
            acts = [act0, act1]
            cbs = [e3h(nc.sbuf_tensor(f"cbs{i}_{half}", [128, HALF], BF16)) for i in range(2)]
            crow = e3h(nc.sbuf_tensor(f"crow_{half}", [1, HALF], BF16))
            sgb = [e3h(nc.sbuf_tensor(f"sgb{i}_{half}", [128, 512], BF16)) for i in range(2)]
            usb = [e3h(nc.sbuf_tensor(f"usb{i}_{half}", [128, 512], BF16)) for i in range(2)]
            if half == 1:
                dvr = op("gpsimd", lambda e: e.dma_start(
                    lnA[:].rearrange("p (h t) -> p h t", h=HC),
                    ln2s[:, HALF:S].rearrange("(h p) t -> p h t", p=128)),
                    waits=[("PE", lnA_pe_last), ("sLNT", dx_lnt_prev)],
                    inc="sLNR", incv=16)
                lnA_ready = ("sLNR", dvr)
            pairs = [(2 * p, 2 * p + 1) for p in range(NE // 2)] + [("shA", "shB")]
            prev_act_pe = 0
            for ip, (ea, eb) in enumerate(pairs):
                shared = ea == "shA"
                if not shared:
                    sc_cb_last = 0
                    for im, ei in enumerate((ea, eb)):
                        dvc = op("gpsimd", lambda e, ei=ei, t0=t0, cr_=crow: e.dma_start(
                            cr_[:, 0:HALF], combTd[ei:ei + 1, t0:t0 + HALF]),
                            waits=[("sCT", dx_comb_done), ("PE", crow_pe)] + half_entry,
                            inc="sCROW", incv=16)
                        for tq in range(HALF // 512):
                            wcb = [("sCROW", dvc), ("SC", bank_sc.get(6, 0)),
                                   ("VE", bank_ve.get(6, 0)),
                                   ("VE", cbs_ve_free[im])] + half_entry
                            op("tensor", lambda e, tq=tq, cr_=crow: e.matmul(
                                bk(6), ones1b[:], cr_[0:1, tq * 512:(tq + 1) * 512],
                                start=True, stop=True), waits=wcb, inc="PE")
                            crow_pe = now("PE")
                            sc_cb_last = op("scalar", lambda e, tq=tq, cb_=cbs[im]: e.copy(
                                cb_[:, tq * 512:(tq + 1) * 512], bk(6)),
                                waits=[("PE", crow_pe)], inc="SC")
                            bank_sc[6] = sc_cb_last
                    cb_ready = ("SC", sc_cb_last)
                else:
                    cb_ready = ("SC", 0)
                njc = J2
                ve_act_last = 0
                for jc in range(njc):
                    slab_src = ([(sw13_slab(jc), sw13_slab(JS2 + jc)),
                                 (sw13_slab(njc + jc), sw13_slab(JS2 + njc + jc))]
                                if shared else
                                [(w13_slab(ea, jc), w13_slab(ea, J2 + jc)),
                                 (w13_slab(eb, jc), w13_slab(eb, J2 + jc))])
                    slabs = []
                    for im in range(2):
                        gsrc, usrc = slab_src[im]
                        sgl, isg = ws.load(gsrc, HC * 128)
                        sul, isu = ws.load(usrc, HC * 128)
                        for tq in range(HALF // 512):
                            bg, bu = (0, 1) if tq == 0 else (2, 3)
                            ring = tq
                            wg = [ws.ready(isg), lnA_ready,
                                  ("SC", bank_sc.get(bg, 0)),
                                  ("VE", bank_ve.get(bg, 0))] + half_entry
                            for hx in range(HC):
                                op("tensor", lambda e, s=sgl, hx=hx, tq=tq, bg=bg: e.matmul(
                                    bk(bg), ws.wsb[s][:, hx * 128:(hx + 1) * 128],
                                    lnA[:, hx * HALF + tq * 512:hx * HALF + (tq + 1) * 512],
                                    start=(hx == 0), stop=(hx == HC - 1)),
                                    waits=(wg if hx == 0 else []),
                                    inc=("PE" if hx == HC - 1 else None))
                            pe_g = now("PE")
                            wu = [ws.ready(isu), ("VE", bank_ve.get(bu, 0)),
                                  ("SC", bank_sc.get(bu, 0))]
                            for hx in range(HC):
                                op("tensor", lambda e, s=sul, hx=hx, tq=tq, bu=bu: e.matmul(
                                    bk(bu), ws.wsb[s][:, hx * 128:(hx + 1) * 128],
                                    lnA[:, hx * HALF + tq * 512:hx * HALF + (tq + 1) * 512],
                                    start=(hx == 0), stop=(hx == HC - 1)),
                                    waits=(wu if hx == 0 else []),
                                    inc=("PE" if hx == HC - 1 else None))
                            pe_u = now("PE")
                            lnA_pe_last = pe_u
                            sc_sg = op("scalar", lambda e, bg=bg, sg_=sgb[ring]: e.activation(
                                sg_[:], bk(bg), AF.Silu),
                                waits=[("PE", pe_g), ("VE", sgb_ve.get(ring, 0))]
                                + half_entry, inc="SC")
                            bank_sc[bg] = sc_sg
                            wuc = [("PE", pe_u), ("VE", usb_ve.get(ring, 0))] + half_entry
                            if shared:
                                ve_uc = op("vector", lambda e, bu=bu, ub_=usb[ring]: e.tensor_copy(
                                    ub_[:], bk(bu)), waits=wuc, inc="VE",
                                    pub=False)
                            else:
                                wuc.append(cb_ready)
                                ve_uc = op("vector",
                                           lambda e, bu=bu, tq=tq, ub_=usb[ring],
                                           cb_=cbs[im]:
                                           e.tensor_tensor(
                                               ub_[:], bk(bu),
                                               cb_[:, tq * 512:(tq + 1) * 512],
                                               ALU.mult), waits=wuc, inc="VE",
                                           pub=False)
                            bank_ve[bu] = ve_uc
                            dsta = acts[im][:, jc * HALF + tq * 512:
                                            jc * HALF + (tq + 1) * 512]
                            wact = [("SC", sc_sg)] + half_entry
                            if jc == 0 and tq == 0:
                                wact.append(("PE", prev_act_pe))
                            ve_a = op("vector", lambda e, d=dsta, sg_=sgb[ring],
                                      ub_=usb[ring]: e.tensor_mul(
                                d, sg_[:], ub_[:]), waits=wact, inc="VE")
                            sgb_ve[ring] = ve_a
                            usb_ve[ring] = ve_a
                            ve_act_last = ve_a
                        ws.done(isg, now("PE"))
                        ws.done(isu, now("PE"))
                if not shared:
                    cbs_ve_free = [ve_act_last, ve_act_last]
                nyj = J2
                for hx in range(HC):
                    if shared:
                        sy_a, isy_a = ws.load(
                            sw2[hx, :, 0:JS2 // 2, :].rearrange("p j k -> p (j k)"),
                            (JS2 // 2) * 128)
                        sy_b, isy_b = ws.load(
                            sw2[hx, :, JS2 // 2:JS2, :].rearrange("p j k -> p (j k)"),
                            (JS2 // 2) * 128)
                    else:
                        sy_a, isy_a = ws.load(w2[ea, hx].rearrange("p j k -> p (j k)"),
                                              J2 * 128)
                        sy_b, isy_b = ws.load(w2[eb, hx].rearrange("p j k -> p (j k)"),
                                              J2 * 128)
                    for tq in range(HALF // 512):
                        by = 4 + 2 * tq + (hx & 1)
                        wy = [ws.ready(isy_a), ws.ready(isy_b), ("VE", ve_act_last),
                              ("VE", bank_ve.get(by, 0)), ("SC", bank_sc.get(by, 0))]
                        first = True
                        for im, syl in ((0, sy_a), (1, sy_b)):
                            for j2x in range(nyj):
                                lastmm = (im == 1 and j2x == nyj - 1)
                                op("tensor",
                                   lambda e, s=syl, j2x=j2x, im=im, tq=tq, by=by,
                                   lm=lastmm, a_=acts[im]: e.matmul(
                                       bk(by), ws.wsb[s][:, j2x * 128:(j2x + 1) * 128],
                                       a_[:, j2x * HALF + tq * 512:
                                          j2x * HALF + (tq + 1) * 512],
                                       start=(im == 0 and j2x == 0), stop=lm),
                                   waits=(wy if first else []),
                                   inc=("PE" if lastmm else None))
                                first = False
                        pe_y = now("PE")
                        dsta = acc[:, hx * HALF + tq * 512:hx * HALF + (tq + 1) * 512]
                        if ip == 0:
                            v = op("scalar", lambda e, d=dsta, by=by: e.copy(d, bk(by)),
                                   waits=[("PE", pe_y), ("VE", acc_read_ve)], inc="SC")
                            bank_sc[by] = v
                        else:
                            v = op("vector", lambda e, d=dsta, by=by: e.tensor_tensor(
                                d, bk(by), d, ALU.add), waits=[("PE", pe_y)], inc="VE",
                                pub=False)
                            bank_ve[by] = v
                    ws.done(isy_a, now("PE"))
                    ws.done(isy_b, now("PE"))
                prev_act_pe = now("PE")
            ve_acc_done = E.fence("vector")
            sc_acc_done = now("SC")
            if half == 1 and "dbg_acc1" in dbg_t:
                for hcx in range(HC):
                    op("gpsimd", lambda e, hx=hcx: e.dma_start(
                        dbg_t["dbg_acc1"][hx * 128:(hx + 1) * 128, :],
                        acc[:, hx * HALF:(hx + 1) * HALF]),
                        waits=[("VE", ve_acc_done), ("SC", sc_acc_done)],
                        inc="sDBG", incv=16)
            if half == 1 and "dbg_lnr" in dbg_t:
                op("gpsimd", lambda e: e.dma_start(
                    dbg_t["dbg_lnr"][:, 0:HC * HALF], lnA[:]),
                    waits=[("VE", ve_acc_done)], inc="sDBG", incv=16)
            if half == 0 and "dbg_acc" in dbg_t:
                for hcx in range(HC):
                    op("gpsimd", lambda e, hx=hcx: e.dma_start(
                        dbg_t["dbg_acc"][hx * 128:(hx + 1) * 128, :],
                        acc[:, hx * HALF:(hx + 1) * HALF]),
                        waits=[("VE", ve_acc_done), ("SC", sc_acc_done)],
                        inc="sDBG", incv=16)

            p3h.close()
            p4h = contextlib.ExitStack()
            xpc = p4h.enter_context(nc.sbuf_tensor(f"xpc_{half}", [128, HC * 128], F32))
            p4_entry = [(s, E.cnt[s]) for s in list(E.cnt)]
            for t4 in range(HALF // 128):
                wdx = [("sX1", dx_x1_done)] + (p4_entry if t4 == 0 else [])
                if dx_out:
                    wdx += [("VE", ve_out_done), ("sOUT", dx_out)]
                dvx = op("gpsimd", lambda e, t4=t4, t0=t0, xp_=xpc: e.dma_start(
                    xp_[:].rearrange("p (h t) -> p h t", h=HC),
                    x1T[:, t0 + t4 * 128:t0 + (t4 + 1) * 128]
                    .rearrange("(h p) t -> p h t", p=128)),
                    waits=wdx, inc="sXPC", incv=16)
                d4 = emit_ln_stats(
                    lambda hx, t4=t4: acc[:, hx * HALF + t4 * 128:
                                          hx * HALF + t4 * 128 + 128],
                    128, 2, 3, 4, 5,
                    [("VE", ve_acc_done), ("SC", sc_acc_done)])
                last_ve = 0
                for hx in range(HC):
                    r = hx % 2
                    w = [("PE", d4["pe_bcast"])] + ([("sXPC", dvx)] if hx == 0 else [])
                    op("vector", lambda e, hx=hx, r=r, t4=t4: e.tensor_sub(
                        fsc[:, r * 512:r * 512 + 128],
                        acc[:, hx * HALF + t4 * 128:hx * HALF + t4 * 128 + 128],
                        bk(4, 0, 128)), waits=w, inc="VE", pub=False)
                    op("vector", lambda e, r=r: e.tensor_mul(
                        fsc[:, r * 512:r * 512 + 128], fsc[:, r * 512:r * 512 + 128],
                        bk(5, 0, 128)), inc="VE", pub=False)
                    last_ve = op("vector", lambda e, hx=hx, r=r, xp_=xpc: e.tensor_add(
                        xp_[:, hx * 128:(hx + 1) * 128], xp_[:, hx * 128:(hx + 1) * 128],
                        fsc[:, r * 512:r * 512 + 128]), inc="VE",
                        pub=(hx == HC - 1))
                ve_out_done = last_ve
                acc_read_ve = now("VE")
                dx_out = op("gpsimd", lambda e, t4=t4, t0=t0, xp_=xpc: e.dma_start(
                    outT[:, t0 + t4 * 128:t0 + (t4 + 1) * 128]
                    .rearrange("(h p) t -> p h t", p=128),
                    xp_[:].rearrange("p (h t) -> p h t", h=HC)),
                    waits=[("VE", last_ve)], inc="sOUT", incv=16)
            for bnk in range(6):
                bank_ve[bnk] = max(bank_ve.get(bnk, 0), now("VE"))
            p4h.close()
        fin = [("sOUT", E.cnt.get("sOUT", 0))]
        if dbg_t:
            fin.append(("sDBG", E.cnt.get("sDBG", 0)))
        op("gpsimd", lambda e: e.engine_nop(), waits=fin, inc=None)
        p3.close()

        block = en(nc.Block())
        E.flush(block)
    return nc


def prep_shared(c, c_attn_w, c_attn_b, c_proj_w, c_proj_b, gate_w, w13, w2, sw13, sw2):
    HC, NE, J2, JS2 = c.HC, c.NE, c.J2, c.JS2
    f = np.ascontiguousarray
    aw = f(c_attn_w.reshape(HC, 128, 3 * HC, 128).transpose(2, 1, 0, 3))
    ab = f(c_attn_b.reshape(3 * HC, 128).T)
    pw = f(c_proj_w.reshape(HC, 128, HC, 128).transpose(2, 1, 0, 3))
    pb = f(c_proj_b.reshape(HC, 128).T)
    gw = f(gate_w.T.reshape(HC, 128, NE))
    w13b = f(w13.reshape(NE, 2 * J2, 128, HC, 128).transpose(0, 1, 4, 3, 2))
    w2b = f(w2.reshape(NE, HC, 128, J2, 128).transpose(0, 1, 4, 3, 2))
    sw13b = f(sw13.reshape(2 * JS2, 128, HC, 128).transpose(0, 3, 2, 1))
    sw2b = f(sw2.reshape(HC, 128, JS2, 128).transpose(0, 3, 2, 1))
    ident = np.eye(128, dtype=np.float32)
    ii, jj = np.meshgrid(np.arange(128), np.arange(128), indexing="ij")
    cmask = np.where(jj <= ii, 0.0, NEGM).astype(np.float32)
    cmaskT = np.ascontiguousarray(cmask.T)
    cstv = f(np.concatenate([ident, cmask, cmaskT], axis=1))
    return {"aw": aw, "ab": ab, "pw": pw, "pb": pb, "gw": gw, "w13": w13b,
            "w2": w2b, "sw13": sw13b, "sw2": sw2b, "cst": cstv}


def kernel(L0_attn_hidden_states, L0_attn_c_attn_w, L0_attn_c_attn_b,
           L0_attn_c_proj_w, L0_attn_c_proj_b, L0_ffn_gate_w, L0_ffn_w13,
           L0_ffn_w2, L0_ffn_sw13, L0_ffn_sw2):
    c = CFG()
    x = np.asarray(L0_attn_hidden_states, dtype=np.float32)
    B = x.shape[0]
    shared = prep_shared(
        c, np.asarray(L0_attn_c_attn_w, np.float32),
        np.asarray(L0_attn_c_attn_b, np.float32),
        np.asarray(L0_attn_c_proj_w, np.float32),
        np.asarray(L0_attn_c_proj_b, np.float32),
        np.asarray(L0_ffn_gate_w, np.float32), np.asarray(L0_ffn_w13, np.float32),
        np.asarray(L0_ffn_w2, np.float32), np.asarray(L0_ffn_sw13, np.float32),
        np.asarray(L0_ffn_sw2, np.float32))
    nc = build_nc(c)
    in_maps = []
    for b in range(B):
        m = dict(shared)
        m["xT"] = np.ascontiguousarray(x[b].T)
        in_maps.append(m)
    res = run_bass_kernel_spmd(nc, in_maps, core_ids=list(range(B)))
    out = np.stack([np.asarray(r["outT"]).T for r in res.results], axis=0)
    return out.astype(np.float32)


# version 21
# speedup vs baseline: 1.1016x; 1.0931x over previous
import contextlib
import math

import numpy as np
import ml_dtypes

import concourse.bass as bass
import concourse.mybir as mybir
from concourse.bass_utils import run_bass_kernel_spmd

F32 = mybir.dt.float32
BF16 = mybir.dt.bfloat16
AF = mybir.ActivationFunctionType
ALU = mybir.AluOpType
AX = mybir.AxisListType.X

NEGM = -1.0e30
EPS = 1e-5
ENGINES = ("sync", "tensor", "scalar", "vector", "gpsimd")


class CFG:
    def __init__(self, D=2048, S=2048, NE=16, TOPK=6, I1=1408, IS=2816):
        self.D, self.S, self.NE, self.TOPK, self.I1, self.IS = D, S, NE, TOPK, I1, IS
        assert D % 128 == 0 and S % 1024 == 0 and I1 % 128 == 0 and IS % 256 == 0
        self.HC = D // 128
        self.NH = self.HC
        self.NT = S // 128
        self.HALF = S // 2
        self.NTH = self.HALF // 128
        self.J2 = I1 // 128
        self.JS2 = IS // 128
        assert IS == 2 * I1, "shared expert must split into two expert-shaped subs"
        assert self.JS2 % 2 == 0
        self.ALPHA = 1.0 / math.sqrt(128.0)


class Emit:

    def __init__(self, nc, stack):
        self.nc = nc
        self.stack = stack
        self.qs = {e: [] for e in ENGINES}
        self.sems = {}
        self.cnt = {}
        self.lastwait = {e: {} for e in ENGINES}
        for s in ("dW", "PE", "SC", "VE", "PO"):
            self.sem(s)

    def sem(self, s):
        if s not in self.sems:
            self.sems[s] = self.stack.enter_context(self.nc.semaphore(s))
            self.cnt[s] = 0
        return self.sems[s]

    strict = False
    strict_phases = None
    phase = "pre"

    def fence(self, eng):
        sem = {"vector": "VE", "scalar": "SC", "gpsimd": "PO", "tensor": "PE"}[eng]
        return self.op(eng, lambda e: e.drain(fusable=True), inc=sem, pub=False)

    def op(self, eng, fn, waits=(), inc=None, incv=1, pub=None):
        s_on = (Emit.strict if Emit.strict_phases is None
                else (Emit.phase in Emit.strict_phases))
        if s_on:
            waits = list(waits) + [(s, v) for s, v in self.cnt.items()]
        lw = self.lastwait[eng]
        eff = []
        for s, v in waits:
            if v and v > lw.get(s, 0):
                self.sem(s)
                eff.append((s, v))
                lw[s] = v
        after = None
        if inc is not None:
            self.sem(inc)
            self.cnt[inc] += incv
            after = self.cnt[inc]
        sems = self.sems
        if pub is None:
            pub = inc in ("SC", "VE", "PO")

        def closure(e, fn=fn, eff=tuple(eff), inc=inc, incv=incv, pub=pub):
            for s, v in eff:
                e.wait_ge(sems[s], v)
            instr = fn(e)
            if inc is not None:
                assert instr is not None, "op with inc must return an instruction"
                if pub:
                    e.drain(fusable=True).then_inc(sems[inc], incv)
                else:
                    instr.then_inc(sems[inc], incv)

        self.qs[eng].append(closure)
        return after

    def now(self, s):
        return self.cnt[s]

    def flush(self, block):
        for e in ENGINES:
            fns = self.qs[e]

            def runner(engine, fns=fns):
                for f in fns:
                    f(engine)

            getattr(block, e)(runner)


class WStream:

    def __init__(self, E, stack, nc, ns, cols, name):
        self.E, self.ns, self.cols = E, ns, cols
        self.wsf = [stack.enter_context(nc.sbuf_tensor(f"{name}f{i}", [128, cols], F32))
                    for i in range(ns)]
        self.wsb = [stack.enter_context(nc.sbuf_tensor(f"{name}b{i}", [128, cols], BF16))
                    for i in range(ns)]
        self.i = 0
        self.cast_done = {}
        self.use_done = {}
        self.sem_of = {"gpsimd": "PO", "scalar": "SC", "vector": "VE"}
        self.rr = ("gpsimd", "vector", "scalar")
        self.mode = None
        self.cache_idx = 0
        self.rec_done = {}

    def load(self, dram_ap, ncols, cast=None):
        E = self.E
        i = self.i
        if cast is None:
            cast = self.rr[i % 3]
        self.i += 1
        slot = i % self.ns
        prev = i - self.ns
        waits = [self.cast_done[prev]] if prev >= 0 else []
        dsem = f"wsS{slot}"
        if self.mode is not None and self.mode[0] == "replay":
            cap = self.mode[1]
            ci = self.cache_idx
            self.cache_idx += 1
            rwaits = []
            if prev >= 0:
                assert prev in self.use_done, f"slab {prev} consumers not recorded"
                rwaits.append(("PE", self.use_done[prev]))
                rwaits.append(self.cast_done[prev])
            dv = E.op("sync",
                      lambda e, s=slot, w=ncols, ci=ci, cap=cap: e.dma_start(
                          self.wsb[s][:, 0:w], cap(ci)[:, 0:w]),
                      waits=rwaits, inc=dsem, incv=16)
            self.cast_done[i] = (dsem, dv)
            return slot, i
        dv = E.op("sync",
                  lambda e, d=dram_ap, s=slot, w=ncols: e.dma_start(self.wsf[s][:, 0:w], d),
                  waits=waits, inc=dsem, incv=16)
        cwaits = [(dsem, dv)]
        if prev >= 0:
            assert prev in self.use_done, f"slab {prev} consumers not recorded"
            cwaits.append(("PE", self.use_done[prev]))
            if prev in self.rec_done:
                cwaits.append(self.rec_done[prev])
        sem = self.sem_of[cast]
        if cast == "scalar":
            fn = lambda e, s=slot, w=ncols: e.copy(self.wsb[s][:, 0:w],
                                                   self.wsf[s][:, 0:w])
        else:
            fn = lambda e, s=slot, w=ncols: e.tensor_copy(self.wsb[s][:, 0:w],
                                                          self.wsf[s][:, 0:w])
        cv = E.op(cast, fn, waits=cwaits, inc=sem)
        self.cast_done[i] = (sem, cv)
        if self.mode is not None and self.mode[0] == "record":
            cap = self.mode[1]
            ci = self.cache_idx
            self.cache_idx += 1
            rsem = f"wcR{slot}"
            rv = E.op("gpsimd",
                      lambda e, s=slot, w=ncols, ci=ci, cap=cap: e.dma_start(
                          cap(ci)[:, 0:w], self.wsb[s][:, 0:w]),
                      waits=[(sem, cv)], inc=rsem, incv=16)
            self.rec_done[i] = (rsem, rv)
        return slot, i

    def ready(self, idx):
        return self.cast_done[idx]

    def done(self, idx, pe_val):
        self.use_done[idx] = pe_val


def build_nc(cfg=None, dbg=()):
    c = cfg or CFG()
    HC, NT, NE, AL = c.HC, c.NT, c.NE, c.ALPHA
    NH, S, D, HALF = c.NH, c.S, c.D, c.HALF
    J2, JS2 = c.J2, c.JS2
    NTC4 = S // 512
    nc = bass.Bass("TRN2", target_bir_lowering=False, detect_race_conditions=False)

    dt = nc.dram_tensor
    xT = dt("xT", [D, S], F32, kind="ExternalInput").ap()
    aw = dt("aw", [3 * NH, 128, HC, 128], F32, kind="ExternalInput").ap()
    ab = dt("ab", [128, 3 * NH], F32, kind="ExternalInput").ap()
    pw = dt("pw", [HC, 128, HC, 128], F32, kind="ExternalInput").ap()
    pb = dt("pb", [128, HC], F32, kind="ExternalInput").ap()
    gw = dt("gw", [HC, 128, NE], F32, kind="ExternalInput").ap()
    w13 = dt("w13", [NE, 2 * J2, 128, HC, 128], F32, kind="ExternalInput").ap()
    w2 = dt("w2", [NE, HC, 128, J2, 128], F32, kind="ExternalInput").ap()
    sw13 = dt("sw13", [2 * JS2, 128, HC, 128], F32, kind="ExternalInput").ap()
    sw2 = dt("sw2", [HC, 128, JS2, 128], F32, kind="ExternalInput").ap()
    cst = dt("cst", [128, 384], F32, kind="ExternalInput").ap()
    outT = dt("outT", [D, S], F32, kind="ExternalOutput").ap()
    x1T = dt("x1T", [D, S], F32, kind="Internal").ap()
    aTs = dt("aTs", [D, S], F32, kind="Internal").ap()
    ln2s = dt("ln2s", [D, S], BF16, kind="Internal").ap()
    combTd = dt("combTd", [NE, S], F32, kind="Internal").ap()
    n_cache = (NE // 2 + 1) * (J2 * 4 + HC * 2)
    nc_a = n_cache // 2
    wcb_a = dt("wcb_a", [nc_a, 128, 2048], BF16, kind="Internal").ap()
    wcb_b = dt("wcb_b", [n_cache - nc_a, 128, 2048], BF16, kind="Internal").ap()

    def wcbd(ci):
        return wcb_a[ci] if ci < nc_a else wcb_b[ci - nc_a]
    dbg_t = {}
    for name, shape, dtp in [
        ("dbg_ln1", [D, S], BF16), ("dbg_attn", [128, NH * S], BF16),
        ("dbg_x1", [D, S], F32), ("dbg_ln2", [D, S], BF16),
        ("dbg_comb", [NE, S], F32), ("dbg_acc", [D, HALF], F32),
        ("dbg_a", [D, S], F32), ("dbg_acc1", [D, HALF], F32),
        ("dbg_lnr", [128, 16384], BF16),
    ]:
        if name in dbg:
            dbg_t[name] = dt(name, shape, dtp, kind="ExternalOutput").ap()

    with contextlib.ExitStack() as st:
        en = st.enter_context
        E = Emit(nc, st)
        op, now = E.op, E.now

        ident = en(nc.sbuf_tensor("ident", [128, 128], F32))
        cmask = en(nc.sbuf_tensor("cmaskS", [128, 128], F32))
        cmaskT = en(nc.sbuf_tensor("cmaskTS", [128, 128], F32))
        ones1b = en(nc.sbuf_tensor("ones1b", [1, 128], BF16))
        ones1f = en(nc.sbuf_tensor("ones1f", [1, 128], F32))
        onescb = en(nc.sbuf_tensor("onescb", [128, 1], BF16))
        onescf = en(nc.sbuf_tensor("onescf", [128, 1], F32))
        epst = en(nc.sbuf_tensor("epst", [128, 1], F32))
        abs_ = en(nc.sbuf_tensor("abs_", [128, 3 * NH], F32))
        pbs = en(nc.sbuf_tensor("pbs", [128, HC], F32))
        gws = en(nc.sbuf_tensor("gws", [128, HC * NE], F32))
        gsum = en(nc.sbuf_tensor("gsum", [1, NE], F32))
        lnA = en(nc.sbuf_tensor("lnA", [128, HC * HALF], BF16))
        sstat = en(nc.sbuf_tensor("sstat", [128, 4 * 512], BF16))
        mrow = en(nc.sbuf_tensor("mrow", [1, 512], F32))
        vrow = en(nc.sbuf_tensor("vrow", [1, 512], F32))
        trow = en(nc.sbuf_tensor("trow", [1, 512], F32))
        rrow = en(nc.sbuf_tensor("rrow", [1, 512], F32))
        fsc = en(nc.sbuf_tensor("fsc", [128, 2 * 512], F32))
        SLABC = max(HC * 128, J2 * 128)
        ws = WStream(E, st, nc, 3, SLABC, "ws")
        PS = en(nc.psum_tensor("PS", [128, 4096], F32))

        def bk(i, lo=0, hi=512):
            return PS[:, i * 512 + lo:i * 512 + hi]

        def bkr(i, lo, hi, p0=0, p1=1):
            return PS[p0:p1, i * 512 + lo:i * 512 + hi]

        for dst, src in [(ident[:], cst[:, 0:128]), (cmask[:], cst[:, 128:256]),
                         (cmaskT[:], cst[:, 256:384]), (abs_[:], ab), (pbs[:], pb)]:
            op("sync", lambda e, d=dst, s=src: e.dma_start(d, s), inc="dW", incv=16)
        d_gw = op("sync", lambda e: e.dma_start(
            gws[:].rearrange("p (h e) -> p h e", h=HC),
            gw.rearrange("h p e -> p h e")), inc="dW", incv=16)
        op("vector", lambda e: e.memset(ones1b[:], 1.0), inc="VE", pub=False)
        op("vector", lambda e: e.memset(ones1f[:], 1.0), inc="VE", pub=False)
        op("vector", lambda e: e.memset(onescb[:], 1.0), inc="VE", pub=False)
        op("vector", lambda e: e.memset(onescf[:], 1.0), inc="VE")
        ve_pre = op("vector", lambda e: e.memset(epst[:], EPS), inc="VE")

        for h in range(HC):
            op("tensor", lambda e, h=h: e.matmul(
                bkr(2, 0, NE), onescf[:], gws[:, h * NE:(h + 1) * NE],
                start=(h == 0), stop=(h == HC - 1)),
                waits=[("dW", d_gw), ("VE", ve_pre)],
                inc="PE" if h == HC - 1 else None)
        sc_gsum = op("scalar", lambda e: e.copy(gsum[:], bkr(2, 0, NE)),
                     waits=[("PE", now("PE"))], inc="SC")

        def emit_ln_stats(src_fn, tw, bsum, bsq, bm, br, waits):
            pe_val = {}
            for h in range(HC):
                r = h % 2
                w = list(waits)
                if h >= 2:
                    w = w + [("PE", pe_val[h - 2])]
                src_ap = src_fn(h)
                v1 = op("vector", lambda e, a=src_ap, r=r: e.tensor_copy(
                    sstat[:, r * 1024:r * 1024 + tw], a), waits=w, inc="VE")
                s1 = op("scalar", lambda e, a=src_ap, r=r: e.activation(
                    sstat[:, r * 1024 + 512:r * 1024 + 512 + tw], a, AF.Square),
                    waits=w, inc="SC")
                op("tensor", lambda e, h=h, r=r: e.matmul(
                    bkr(bsum, 0, tw), onescb[:], sstat[:, r * 1024:r * 1024 + tw],
                    start=(h == 0), stop=(h == HC - 1)),
                    waits=[("VE", v1)], inc="PE")
                op("tensor", lambda e, h=h, r=r: e.matmul(
                    bkr(bsq, 0, tw), onescb[:],
                    sstat[:, r * 1024 + 512:r * 1024 + 512 + tw],
                    start=(h == 0), stop=(h == HC - 1)),
                    waits=[("SC", s1)], inc="PE")
                pe_val[h] = now("PE")
            pe_sums = now("PE")
            op("vector", lambda e: e.tensor_scalar(
                mrow[:, 0:tw], bkr(bsum, 0, tw), 1.0 / D, None, ALU.mult),
                waits=[("PE", pe_sums)], inc="VE")
            op("vector", lambda e: e.tensor_scalar(
                vrow[:, 0:tw], bkr(bsq, 0, tw), 1.0 / D, None, ALU.mult),
                inc="VE")
            op("vector", lambda e: e.tensor_mul(trow[:, 0:tw], mrow[:, 0:tw],
                                                mrow[:, 0:tw]), inc="VE", pub=False)
            ve_m = op("vector", lambda e: e.tensor_sub(vrow[:, 0:tw], vrow[:, 0:tw],
                                                       trow[:, 0:tw]), inc="VE")
            op("scalar", lambda e: e.activation(rrow[:, 0:tw], vrow[:, 0:tw], AF.Ln,
                                                bias=epst[0:1, :]),
               waits=[("VE", ve_m)], inc="SC")
            sc_r = op("scalar", lambda e: e.activation(rrow[:, 0:tw], rrow[:, 0:tw],
                                                       AF.Exp, scale=-0.5), inc="SC")
            op("tensor", lambda e: e.matmul(bk(bm, 0, tw), ones1f[:], mrow[:, 0:tw],
                                            start=True, stop=True),
               waits=[("VE", ve_m)], inc="PE")
            pe_bc = op("tensor", lambda e: e.matmul(bk(br, 0, tw), ones1f[:],
                                                    rrow[:, 0:tw], start=True, stop=True),
                       waits=[("SC", sc_r)], inc="PE")
            return {"pe_bcast": pe_bc, "sc_r": sc_r, "ve_m": ve_m, "pe_sums": pe_sums}

        Emit.phase = "p0"
        ln1_stack = contextlib.ExitStack()
        ln1b = ln1_stack.enter_context(
            nc.sbuf_tensor("ln1b", [128, HC * HALF], BF16, side="right"))

        def ln1_ap(hcx, t0, twid):
            if t0 < HALF:
                assert t0 + twid <= HALF
                return lnA[:, hcx * HALF + t0:hcx * HALF + t0 + twid]
            return ln1b[:, hcx * HALF + (t0 - HALF):hcx * HALF + (t0 - HALF) + twid]

        with nc.sbuf_tensor("xc0", [128, HC * 512], F32) as xc0:
            last_ve = 0
            for tc in range(NTC4):
                wd = [("VE", ve_pre)]
                if tc > 0:
                    wd += [("VE", now("VE")), ("SC", now("SC"))]
                dv = op("gpsimd", lambda e, tc=tc: e.dma_start(
                    xc0[:].rearrange("p (h t) -> p h t", h=HC),
                    xT[:, tc * 512:(tc + 1) * 512].rearrange("(h p) t -> p h t", p=128)),
                    waits=wd, inc="sXC0", incv=16)
                d0 = emit_ln_stats(lambda h: xc0[:, h * 512:(h + 1) * 512], 512,
                                   2, 3, 4, 5, [("sXC0", dv), ("SC", sc_gsum)])
                for h in range(HC):
                    r = h % 2
                    op("vector", lambda e, h=h, r=r: e.tensor_sub(
                        fsc[:, r * 512:(r + 1) * 512], xc0[:, h * 512:(h + 1) * 512],
                        bk(4, 0, 512)), waits=[("PE", d0["pe_bcast"])], inc="VE",
                        pub=False)
                    last_ve = op("vector", lambda e, h=h, r=r, tc=tc: e.tensor_mul(
                        ln1_ap(h, tc * 512, 512), fsc[:, r * 512:(r + 1) * 512],
                        bk(5, 0, 512)), inc="VE", pub=False)
            ve_ln1_done = E.fence("vector")
        if "dbg_ln1" in dbg_t:
            for hcx in range(HC):
                op("gpsimd", lambda e, h=hcx: e.dma_start(
                    dbg_t["dbg_ln1"][h * 128:(h + 1) * 128, 0:HALF],
                    lnA[:, h * HALF:(h + 1) * HALF]),
                    waits=[("VE", ve_ln1_done)], inc="sDBG", incv=16)
                op("gpsimd", lambda e, h=hcx: e.dma_start(
                    dbg_t["dbg_ln1"][h * 128:(h + 1) * 128, HALF:S],
                    ln1b[:, h * HALF:(h + 1) * HALF]), inc="sDBG", incv=16)

        Emit.phase = "p1"
        p1 = contextlib.ExitStack()
        e1 = p1.enter_context
        out_all = e1(nc.sbuf_tensor("out_all", [128, NH * S], BF16))
        p1s = contextlib.ExitStack()
        e1s = p1s.enter_context
        qT = e1s(nc.sbuf_tensor("qT", [128, S], BF16))
        kT = e1s(nc.sbuf_tensor("kT", [128, S], BF16))
        vS = e1s(nc.sbuf_tensor("vS", [128, S], BF16))
        frows = e1s(nc.sbuf_tensor("frows", [1, S], BF16))
        frows2 = e1s(nc.sbuf_tensor("frows2", [1, 512], BF16))
        PTs = [e1s(nc.sbuf_tensor(f"PT{i}", [128, 512], BF16)) for i in range(2)]
        cols = e1s(nc.sbuf_tensor("cols", [128, 8], F32))

        sc_head_tail = 0
        ve_bt_tail = 0
        pt_pe = [0, 0]
        pe_qkv_last = 0
        for h in range(NH):
            Emit.phase = "p1qkv"
            sq, isq = ws.load(aw[h].rearrange("p h j -> p (h j)"), HC * 128)
            sk, isk = ws.load(aw[NH + h].rearrange("p h j -> p (h j)"), HC * 128)
            sv, isv = ws.load(aw[2 * NH + h].rearrange("p h j -> p (h j)"), HC * 128)
            sc_evac = {}
            for which, slab, islab, dstT, bias_i in (
                    ("q", sq, isq, qT, h), ("k", sk, isk, kT, NH + h)):
                for tc in range(NTC4):
                    bq = tc % 2
                    w = [ws.ready(islab), ("VE", ve_ln1_done), ("SC", sc_head_tail)]
                    if (which, tc - 2) in sc_evac:
                        w.append(("SC", sc_evac[(which, tc - 2)]))
                    if which == "k":
                        prevq = (S // 512) - 2 + (tc % 2)
                        if ("q", prevq) in sc_evac:
                            w.append(("SC", sc_evac[("q", prevq)]))
                    for hx in range(HC):
                        op("tensor", lambda e, s=slab, hx=hx, tc=tc, bq=bq: e.matmul(
                            bk(bq), ws.wsb[s][:, hx * 128:(hx + 1) * 128],
                            ln1_ap(hx, tc * 512, 512),
                            start=(hx == 0), stop=(hx == HC - 1)),
                            waits=(w if hx == 0 else []),
                            inc=("PE" if hx == HC - 1 else None))
                    pe_g = now("PE")
                    sc_evac[(which, tc)] = op(
                        "scalar", lambda e, d=dstT, tc=tc, bq=bq, bi=bias_i: e.activation(
                            d[:, tc * 512:(tc + 1) * 512], bk(bq), AF.Identity,
                            bias=abs_[:, bi:bi + 1]),
                        waits=[("PE", pe_g)], inc="SC")
            ws.done(isq, now("PE"))
            ws.done(isk, now("PE"))
            for tt in range(NT):
                bv = 2 + (tt // 4) % 2
                reg = tt % 4
                w = [ws.ready(isv), ("VE", ve_ln1_done), ("SC", sc_head_tail)]
                if ("v", tt // 4 - 2) in sc_evac:
                    w.append(("SC", sc_evac[("v", tt // 4 - 2)]))
                for hx in range(HC):
                    op("tensor", lambda e, hx=hx, tt=tt, bv=bv, reg=reg, s=sv: e.matmul(
                        bk(bv, reg * 128, reg * 128 + 128),
                        ln1_ap(hx, tt * 128, 128),
                        ws.wsb[s][:, hx * 128:(hx + 1) * 128],
                        start=(hx == 0), stop=(hx == HC - 1)),
                        waits=(w if hx == 0 else []),
                        inc=("PE" if hx == HC - 1 else None))
                if reg == 3:
                    pe_g = now("PE")
                    sc_evac[("v", tt // 4)] = op(
                        "scalar", lambda e, tt=tt, bv=bv: e.copy(
                            vS[:, (tt - 3) * 128:(tt + 1) * 128], bk(bv)),
                        waits=[("PE", pe_g)], inc="SC")
            ws.done(isv, now("PE"))
            sc_qkv_done = sc_evac[("v", NT // 4 - 1)]
            pe_qkv_last = now("PE")

            Emit.phase = "p1a"
            sc_frows = {}
            ve_max_v = {}
            pend = None
            for qi in range(NT):
                buf = qi % 2
                base = buf * 2048
                kw = (qi + 1) * 128
                r = qi % 4
                w = [("SC", sc_qkv_done), ("dW", d_gw), ("VE", ve_bt_tail)]
                if qi - 2 in ve_max_v:
                    w.append(("VE", ve_max_v[qi - 2]))
                offs = list(range(0, kw, 512))[::-1]
                first = True
                for off in offs:
                    wd = min(512, kw - off)
                    wc = list(w) if first else []
                    if off == 0:
                        wc = wc + [("SC", sc_frows.get(qi - 2, 0))]
                    op("tensor", lambda e, qi=qi, off=off, wd=wd, base=base: e.matmul(
                        PS[:, base + off:base + off + wd],
                        qT[:, qi * 128:(qi + 1) * 128], kT[:, off:off + wd],
                        start=True, stop=True),
                        waits=wc,
                        inc=("PE" if off == 0 else None))
                    first = False
                pe_row = now("PE")
                op("vector", lambda e, qi=qi, base=base, kw=kw: e.tensor_tensor(
                    PS[:, base + qi * 128:base + kw], PS[:, base + qi * 128:base + kw],
                    cmask[:], ALU.add), waits=[("PE", pe_row)], inc="VE", pub=False)
                ve_mx = op("vector", lambda e, base=base, kw=kw, r=r: e.reduce_max(
                    cols[:, r:r + 1], PS[:, base:base + kw], axis=AX, negate=True),
                    inc="VE")
                ve_max_v[qi] = ve_mx
                if pend is not None:
                    pqi, pr, pvf = pend
                    tbase = (pqi % 2) * 2048
                    pe_tr = op("tensor", lambda e, tb=tbase, r=pr: e.transpose(
                        PS[0:1, tb:tb + 128], cols[:, r:r + 1], ident[:]),
                        waits=[("VE", pvf), ("SC", sc_frows.get(pqi - 2, 0))],
                        inc="PE")
                    sc_frows[pqi] = op("scalar", lambda e, tb=tbase, qi=pqi: e.copy(
                        frows[0:1, qi * 128:(qi + 1) * 128], PS[0:1, tb:tb + 128]),
                        waits=[("PE", pe_tr)], inc="SC")
                pend = (qi, r, ve_mx)
            pqi, pr, pvf = pend
            tbase = (pqi % 2) * 2048
            pe_tr = op("tensor", lambda e, tb=tbase, r=pr: e.transpose(
                PS[0:1, tb:tb + 128], cols[:, r:r + 1], ident[:]),
                waits=[("VE", pvf)], inc="PE")
            sc_frows[pqi] = op("scalar", lambda e, tb=tbase, qi=pqi: e.copy(
                frows[0:1, qi * 128:(qi + 1) * 128], PS[0:1, tb:tb + 128]),
                waits=[("PE", pe_tr)], inc="SC")
            sc_pa_done = sc_frows[NT - 1]

            Emit.phase = "p1b"
            gslot = 0
            stp_sc = [0, 0]
            sc_outev = 0
            ve_outev = 0
            sc_rb = 0
            pe_pv = 0
            for qi in range(NT):
                nk = qi + 1
                for g0 in range(0, nk, 4):
                    gwd = min(4, nk - g0)
                    slot = gslot % 2
                    gslot += 1
                    base = (4 + slot) * 512
                    w = [("SC", sc_pa_done), ("SC", stp_sc[slot]), ("SC", sc_rb)]
                    first = True
                    for j in range(gwd):
                        kc = g0 + j
                        op("tensor", lambda e, j=j, kc=kc, qi=qi, base=base: e.matmul(
                            PS[:, base + j * 128:base + (j + 1) * 128],
                            kT[:, kc * 128:(kc + 1) * 128],
                            qT[:, qi * 128:(qi + 1) * 128],
                            start=True, stop=False),
                            waits=(w if first else []), inc=None)
                        first = False
                        op("tensor", lambda e, j=j, qi=qi, base=base: e.matmul(
                            PS[:, base + j * 128:base + (j + 1) * 128],
                            ones1b[:], frows[0:1, qi * 128:(qi + 1) * 128],
                            start=False, stop=True),
                            inc=("PE" if j == gwd - 1 else None))
                    pe_grp = now("PE")
                    wexp = [("PE", pe_grp), ("PE", pt_pe[slot])]
                    if g0 <= qi < g0 + gwd:
                        jd = qi - g0
                        ve_d = op("vector", lambda e, jd=jd, base=base: e.tensor_tensor(
                            PS[:, base + jd * 128:base + (jd + 1) * 128],
                            PS[:, base + jd * 128:base + (jd + 1) * 128],
                            cmaskT[:], ALU.add), waits=[("PE", pe_grp)], inc="VE")
                        wexp.append(("VE", ve_d))
                    sc_e = op("scalar", lambda e, slot=slot, base=base, gwd=gwd: e.activation(
                        PTs[slot][:, 0:gwd * 128], PS[:, base:base + gwd * 128],
                        AF.Exp, scale=AL), waits=wexp, inc="SC")
                    stp_sc[slot] = sc_e
                    wpv = [("SC", sc_e), ("SC", sc_outev), ("VE", ve_outev)]
                    for j in range(gwd):
                        kc = g0 + j
                        op("tensor", lambda e, j=j, kc=kc, qi=qi, slot=slot: e.matmul(
                            bk(6, (qi % 4) * 128, (qi % 4) * 128 + 128),
                            vS[:, kc * 128:(kc + 1) * 128],
                            PTs[slot][:, j * 128:(j + 1) * 128],
                            start=(kc == 0), stop=(kc == qi)),
                            waits=(wpv if j == 0 else []), inc=None)
                        op("tensor", lambda e, j=j, kc=kc, qi=qi, slot=slot: e.matmul(
                            PS[0:1, 3584 + (qi % 4) * 128:3584 + (qi % 4) * 128 + 128],
                            onescb[:], PTs[slot][:, j * 128:(j + 1) * 128],
                            start=(kc == 0), stop=(kc == qi)),
                            inc=("PE" if kc == qi else None))
                    pt_pe[slot] = now("PE")
                pe_pv = now("PE")
                if qi % 4 == 3:
                    op("vector", lambda e: e.reciprocal(
                        rrow[:, 0:512], PS[0:1, 3584:3584 + 512]),
                        waits=[("PE", pe_pv)], inc="VE")
                    ve_rc = op("vector", lambda e: e.tensor_copy(
                        frows2[0:1, 0:512], rrow[:, 0:512]), inc="VE")
                    pe_rb = op("tensor", lambda e: e.matmul(
                        PS[:, 3584:3584 + 512], ones1b[:], frows2[0:1, 0:512],
                        start=True, stop=True), waits=[("VE", ve_rc)], inc="PE")
                    sc_rb = op("scalar", lambda e: e.copy(
                        fsc[:, 512:1024], PS[:, 3584:3584 + 512]),
                        waits=[("PE", pe_rb)], inc="SC")
                    ve_outev = op("vector", lambda e: e.tensor_mul(
                        fsc[:, 0:512], bk(6), fsc[:, 512:1024]),
                        waits=[("SC", sc_rb), ("SC", sc_outev)], inc="VE")
                    sc_outev = op("scalar", lambda e, h=h, qi=qi: e.activation(
                        out_all[:, h * S + (qi - 3) * 128:h * S + (qi + 1) * 128],
                        fsc[:, 0:512], AF.Identity,
                        bias=abs_[:, 2 * NH + h:2 * NH + h + 1]),
                        waits=[("VE", ve_outev)], inc="SC")
            sc_head_tail = sc_outev
            ve_bt_tail = ve_outev
        sc_attn_done = sc_head_tail
        pe_ln1_free = pe_qkv_last
        if "dbg_attn" in dbg_t:
            op("gpsimd", lambda e: e.dma_start(dbg_t["dbg_attn"], out_all[:]),
               waits=[("SC", sc_attn_done)], inc="sDBG", incv=16)

        Emit.phase = "p2a"
        p1s.close()
        asb = e1(nc.sbuf_tensor("asb", [128, 2 * 512], F32))
        asb_hist = {}
        for dac in range(HC):
            sp, isp = ws.load(pw[dac].rearrange("p h j -> p (h j)"), HC * 128)
            for tcw in range(NTC4):
                gi = dac * NTC4 + tcw
                bq = tcw % 2
                w = [ws.ready(isp), ("SC", sc_attn_done)]
                if gi - 2 in asb_hist:
                    w.append(("SC", asb_hist[gi - 2][0]))
                for dcx in range(HC):
                    op("tensor", lambda e, s=sp, dcx=dcx, tcw=tcw, bq=bq: e.matmul(
                        bk(bq), ws.wsb[s][:, dcx * 128:(dcx + 1) * 128],
                        out_all[:, dcx * S + tcw * 512:dcx * S + (tcw + 1) * 512],
                        start=(dcx == 0), stop=(dcx == HC - 1)),
                        waits=(w if dcx == 0 else []),
                        inc=("PE" if dcx == HC - 1 else None))
                pe_g = now("PE")
                ring = gi % 2
                wsc = [("PE", pe_g)]
                if gi - 2 in asb_hist:
                    wsc.append((f"sASB{ring}", asb_hist[gi - 2][1]))
                sc_a = op("scalar", lambda e, ring=ring, bq=bq, dac=dac: e.activation(
                    asb[:, ring * 512:(ring + 1) * 512], bk(bq), AF.Identity,
                    bias=pbs[:, dac:dac + 1]), waits=wsc, inc="SC")
                dxv = op("gpsimd", lambda e, ring=ring, dac=dac, tcw=tcw: e.dma_start(
                    aTs[dac * 128:(dac + 1) * 128, tcw * 512:(tcw + 1) * 512],
                    asb[:, ring * 512:(ring + 1) * 512]),
                    waits=[("SC", sc_a)], inc=f"sASB{ring}", incv=16)
                asb_hist[gi] = (sc_a, dxv)
            ws.done(isp, now("PE"))
        p2a_done = [("sASB0", E.cnt.get("sASB0", 0)), ("sASB1", E.cnt.get("sASB1", 0))]
        if "dbg_a" in dbg_t:
            op("gpsimd", lambda e: e.dma_start(dbg_t["dbg_a"], aTs),
               waits=p2a_done, inc="sDBG", incv=16)
        p1.close()
        ln1_stack.close()

        Emit.phase = "p2b"
        ve_lnA_w = 0
        dx_x1_done = 0
        dx_comb_done = 0
        p2 = contextlib.ExitStack()
        e2 = p2.enter_context
        ac = e2(nc.sbuf_tensor("ac", [128, HC * 128], F32))
        xc2 = e2(nc.sbuf_tensor("xc2", [128, HC * 128], F32))
        lnt = e2(nc.sbuf_tensor("lnt", [128, HC * 128], BF16))
        gl = e2(nc.sbuf_tensor("gl", [128, NE], F32))
        gp = e2(nc.sbuf_tensor("gp", [128, NE], F32))
        gwk = e2(nc.sbuf_tensor("gwk", [128, NE], F32))
        gm = e2(nc.sbuf_tensor("gm", [128, 4], F32))
        r2c = e2(nc.sbuf_tensor("r2c", [128, 1], F32))
        ctile = e2(nc.sbuf_tensor("ctile", [NE, 128], F32))
        n2row = e2(nc.sbuf_tensor("n2row", [1, 128], F32))

        prev_ct_sc = 0
        prev_pe_pg = 0
        dx_lnt_prev = 0
        dx_x1 = 0
        for tc in range(NT):
            wdma = list(p2a_done)
            if tc > 0:
                wdma += [("VE", now("VE")), ("SC", now("SC")), ("PE", now("PE"))]
            dva = op("gpsimd", lambda e, tc=tc: e.dma_start(
                ac[:].rearrange("p (h t) -> p h t", h=HC),
                aTs[:, tc * 128:(tc + 1) * 128].rearrange("(h p) t -> p h t", p=128)),
                waits=wdma, inc="sAC", incv=16)
            wdmx = [("sX1", dx_x1)] if tc > 0 else []
            dvx = op("gpsimd", lambda e, tc=tc: e.dma_start(
                xc2[:].rearrange("p (h t) -> p h t", h=HC),
                xT[:, tc * 128:(tc + 1) * 128].rearrange("(h p) t -> p h t", p=128)),
                waits=wdmx, inc="sXC2", incv=16)
            d1 = emit_ln_stats(lambda hx: ac[:, hx * 128:(hx + 1) * 128], 128,
                               2, 3, 4, 5, [("sAC", dva)])
            last_ve = 0
            for hx in range(HC):
                r = hx % 2
                w = [("PE", d1["pe_bcast"])] + ([("sXC2", dvx)] if hx == 0 else [])
                op("vector", lambda e, hx=hx, r=r: e.tensor_sub(
                    fsc[:, r * 512:r * 512 + 128], ac[:, hx * 128:(hx + 1) * 128],
                    bk(4, 0, 128)), waits=w, inc="VE", pub=False)
                op("vector", lambda e, r=r: e.tensor_mul(
                    fsc[:, r * 512:r * 512 + 128], fsc[:, r * 512:r * 512 + 128],
                    bk(5, 0, 128)), inc="VE", pub=False)
                last_ve = op("vector", lambda e, hx=hx, r=r: e.tensor_add(
                    xc2[:, hx * 128:(hx + 1) * 128], xc2[:, hx * 128:(hx + 1) * 128],
                    fsc[:, r * 512:r * 512 + 128]), inc="VE",
                    pub=(hx == HC - 1))
            dx_x1 = op("gpsimd", lambda e, tc=tc: e.dma_start(
                x1T[:, tc * 128:(tc + 1) * 128].rearrange("(h p) t -> p h t", p=128),
                xc2[:].rearrange("p (h t) -> p h t", h=HC)),
                waits=[("VE", last_ve)], inc="sX1", incv=16)
            dx_x1_done = dx_x1
            d2 = emit_ln_stats(lambda hx: xc2[:, hx * 128:(hx + 1) * 128], 128,
                               2, 3, 6, 7, [("VE", last_ve)])
            last_ve2 = 0
            half1 = tc * 128 >= HALF
            for hx in range(HC):
                r = hx % 2
                w = [("PE", d2["pe_bcast"])]
                if hx == 0:
                    if half1 and dx_lnt_prev:
                        w.append(("sLNT", dx_lnt_prev))
                    if not half1 and tc == 0:
                        w.append(("PE", pe_ln1_free))
                op("vector", lambda e, hx=hx, r=r: e.tensor_sub(
                    fsc[:, r * 512:r * 512 + 128], xc2[:, hx * 128:(hx + 1) * 128],
                    bk(6, 0, 128)), waits=w, inc="VE", pub=False)
                dst = (lnt[:, hx * 128:(hx + 1) * 128] if half1 else
                       lnA[:, hx * HALF + tc * 128:hx * HALF + (tc + 1) * 128])
                last_ve2 = op("vector", lambda e, d=dst, r=r: e.tensor_mul(
                    d, fsc[:, r * 512:r * 512 + 128], bk(7, 0, 128)), inc="VE",
                    pub=(hx == HC - 1))
            if half1:
                dx_lnt_prev = op("gpsimd", lambda e, tc=tc: e.dma_start(
                    ln2s[:, tc * 128:(tc + 1) * 128].rearrange("(h p) t -> p h t", p=128),
                    lnt[:].rearrange("p (h t) -> p h t", h=HC)),
                    waits=[("VE", last_ve2)], inc="sLNT", incv=16)
            ve_lnA_w = max(ve_lnA_w, last_ve2)
            for hx in range(HC):
                w = [("VE", last_ve), ("VE", d2["ve_m"])] if hx == 0 else []
                op("tensor", lambda e, hx=hx: e.matmul(
                    bk(3, 256, 256 + NE), xc2[:, hx * 128:(hx + 1) * 128],
                    gws[:, hx * NE:(hx + 1) * NE], start=(hx == 0), stop=False),
                    waits=w, inc=None)
            ve_n2 = op("vector", lambda e: e.tensor_scalar(
                n2row[:], mrow[:, 0:128], -1.0, None, ALU.mult),
                waits=[("VE", d2["ve_m"]), ("PE", prev_pe_pg)], inc="VE")
            pe_pg = op("tensor", lambda e: e.matmul(
                bk(3, 256, 256 + NE), n2row[:], gsum[:], start=False, stop=True),
                waits=[("VE", ve_n2), ("SC", sc_gsum)], inc="PE")
            prev_pe_pg = pe_pg
            pe_r2 = op("tensor", lambda e: e.transpose(
                bk(1, 0, 1), rrow[0:1, 0:128], ident[0:1, 0:1]),
                waits=[("SC", d2["sc_r"]), ("SC", prev_ct_sc)], inc="PE")
            sc_r2 = op("scalar", lambda e: e.copy(r2c[:], bk(1, 0, 1)),
                       waits=[("PE", pe_r2)], inc="SC")
            op("vector", lambda e: e.tensor_scalar(
                gl[:], bk(3, 256, 256 + NE), r2c[:], None, ALU.mult),
                waits=[("PE", pe_pg), ("SC", sc_r2)], inc="VE", pub=False)
            ve_gm = op("vector", lambda e: e.reduce_max(gm[:, 0:1], gl[:], axis=AX,
                                                        negate=True), inc="VE")
            sc_gp = op("scalar", lambda e: e.activation(
                gp[:], gl[:], AF.Exp, bias=gm[:, 0:1], accum_out=gm[:, 1:2]),
                waits=[("VE", ve_gm)], inc="SC")
            op("vector", lambda e: e.reciprocal(gm[:, 2:3], gm[:, 1:2]),
               waits=[("SC", sc_gp)], inc="VE")
            op("vector", lambda e: e.tensor_scalar(
                gp[:], gp[:], gm[:, 2:3], None, ALU.mult), inc="VE")
            op("vector", lambda e: e.tensor_copy(gwk[:], gp[:]), inc="VE")
            for _ in range(c.TOPK):
                op("vector", lambda e: e.reduce_max(gm[:, 3:4], gwk[:], axis=AX),
                   inc="VE")
                op("vector", lambda e: e.tensor_single_scalar(
                    gl[:], gwk[:], gm[:, 3:4], ALU.is_ge), inc="VE")
                op("vector", lambda e: e.scalar_tensor_tensor(
                    gwk[:], gl[:], -2.0, gwk[:], ALU.mult, ALU.add), inc="VE")
            op("vector", lambda e: e.tensor_sub(gl[:], gp[:], gwk[:]), inc="VE")
            op("vector", lambda e: e.scalar_tensor_tensor(
                gl[:], gl[:], 0.5, gp[:], ALU.mult, ALU.mult), inc="VE")
            op("vector", lambda e: e.reduce_sum(gm[:, 3:4], gl[:], axis=AX), inc="VE")
            op("vector", lambda e: e.tensor_scalar(
                gm[:, 3:4], gm[:, 3:4], 1e-20, None, ALU.add), inc="VE")
            op("vector", lambda e: e.reciprocal(gm[:, 2:3], gm[:, 3:4]), inc="VE")
            ve_cb = op("vector", lambda e: e.tensor_scalar(
                gl[:], gl[:], gm[:, 2:3], None, ALU.mult), inc="VE")
            pe_ct = op("tensor", lambda e: e.transpose(
                bkr(1, 128, 256, 0, NE), gl[:, 0:NE], ident[:]),
                waits=[("VE", ve_cb)], inc="PE")
            sc_ct = op("scalar", lambda e: e.copy(ctile[:], bkr(1, 128, 256, 0, NE)),
                       waits=[("PE", pe_ct), ("sCT", dx_comb_done)], inc="SC")
            dx_ct = op("gpsimd", lambda e, tc=tc: e.dma_start(
                combTd[:, tc * 128:(tc + 1) * 128], ctile[:]),
                waits=[("SC", sc_ct)], inc="sCT", incv=16)
            prev_ct_sc = sc_ct
            dx_comb_done = dx_ct
            prev_ac_readers = [("VE", last_ve), ("SC", now("SC"))]
            prev_xc2_readers = [("VE", last_ve2), ("PE", pe_pg)]
        ve_p2_done = now("VE")
        sc_p2_done = now("SC")
        p2.close()
        if "dbg_x1" in dbg_t:
            op("gpsimd", lambda e: e.dma_start(dbg_t["dbg_x1"], x1T),
               waits=[("sX1", dx_x1_done)], inc="sDBG", incv=16)
        if "dbg_ln2" in dbg_t:
            for hcx in range(HC):
                op("gpsimd", lambda e, hx=hcx: e.dma_start(
                    dbg_t["dbg_ln2"][hx * 128:(hx + 1) * 128, 0:HALF],
                    lnA[:, hx * HALF:(hx + 1) * HALF]),
                    waits=[("VE", ve_p2_done)], inc="sDBG", incv=16)
            op("gpsimd", lambda e: e.dma_start(
                dbg_t["dbg_ln2"][:, HALF:S], ln2s[:, HALF:S]),
                waits=[("sLNT", dx_lnt_prev)], inc="sDBG", incv=16)
        if "dbg_comb" in dbg_t:
            op("gpsimd", lambda e: e.dma_start(dbg_t["dbg_comb"], combTd),
               waits=[("sCT", dx_comb_done)], inc="sDBG", incv=16)

        Emit.phase = "p3"
        acc = en(nc.sbuf_tensor("acc", [128, HC * HALF], F32, side="right"))
        p3 = contextlib.ExitStack()
        e3 = p3.enter_context

        def w13_slab(ei, jc):
            return w13[ei, jc].rearrange("p h j -> p (h j)")

        def sw13_slab(jc):
            return sw13[jc].rearrange("p h j -> p (h j)")

        bank_sc = {}
        bank_ve = {}
        sgb_ve = {}
        usb_ve = {}
        crow_pe = 0
        cbs_ve_free = [0, 0]
        acc_read_ve = 0
        lnA_pe_last = 0
        ve_out_done = 0
        dx_out = 0
        lnA_ready = ("VE", ve_lnA_w)

        for half in range(2):
            t0 = half * HALF
            ws.mode = (("record" if half == 0 else "replay"), wcbd)
            ws.cache_idx = 0
            half_entry = [(s, E.cnt[s]) for s in list(E.cnt)]
            p3h = contextlib.ExitStack()
            e3h = p3h.enter_context
            act0 = e3h(nc.sbuf_tensor(f"act0_{half}", [128, J2 * HALF], BF16))
            act1 = e3h(nc.sbuf_tensor(f"act1_{half}", [128, J2 * HALF], BF16))
            acts = [act0, act1]
            cbs = [e3h(nc.sbuf_tensor(f"cbs{i}_{half}", [128, HALF], BF16)) for i in range(2)]
            crow = e3h(nc.sbuf_tensor(f"crow_{half}", [1, HALF], BF16))
            sgb = [e3h(nc.sbuf_tensor(f"sgb{i}_{half}", [128, 512], BF16)) for i in range(2)]
            usb = [e3h(nc.sbuf_tensor(f"usb{i}_{half}", [128, 512], BF16)) for i in range(2)]
            if half == 1:
                dvr = op("gpsimd", lambda e: e.dma_start(
                    lnA[:].rearrange("p (h t) -> p h t", h=HC),
                    ln2s[:, HALF:S].rearrange("(h p) t -> p h t", p=128)),
                    waits=[("PE", lnA_pe_last), ("sLNT", dx_lnt_prev)],
                    inc="sLNR", incv=16)
                lnA_ready = ("sLNR", dvr)
            pairs = [(2 * p, 2 * p + 1) for p in range(NE // 2)] + [("shA", "shB")]
            prev_act_pe = 0
            for ip, (ea, eb) in enumerate(pairs):
                shared = ea == "shA"
                if not shared:
                    sc_cb_last = 0
                    for im, ei in enumerate((ea, eb)):
                        dvc = op("gpsimd", lambda e, ei=ei, t0=t0, cr_=crow: e.dma_start(
                            cr_[:, 0:HALF], combTd[ei:ei + 1, t0:t0 + HALF]),
                            waits=[("sCT", dx_comb_done), ("PE", crow_pe)] + half_entry,
                            inc="sCROW", incv=16)
                        for tq in range(HALF // 512):
                            wcb = [("sCROW", dvc), ("SC", bank_sc.get(6, 0)),
                                   ("VE", bank_ve.get(6, 0)),
                                   ("VE", cbs_ve_free[im])] + half_entry
                            op("tensor", lambda e, tq=tq, cr_=crow: e.matmul(
                                bk(6), ones1b[:], cr_[0:1, tq * 512:(tq + 1) * 512],
                                start=True, stop=True), waits=wcb, inc="PE")
                            crow_pe = now("PE")
                            sc_cb_last = op("scalar", lambda e, tq=tq, cb_=cbs[im]: e.copy(
                                cb_[:, tq * 512:(tq + 1) * 512], bk(6)),
                                waits=[("PE", crow_pe)], inc="SC")
                            bank_sc[6] = sc_cb_last
                    cb_ready = ("SC", sc_cb_last)
                else:
                    cb_ready = ("SC", 0)
                njc = J2
                ve_act_last = 0
                for jc in range(njc):
                    slab_src = ([(sw13_slab(jc), sw13_slab(JS2 + jc)),
                                 (sw13_slab(njc + jc), sw13_slab(JS2 + njc + jc))]
                                if shared else
                                [(w13_slab(ea, jc), w13_slab(ea, J2 + jc)),
                                 (w13_slab(eb, jc), w13_slab(eb, J2 + jc))])
                    slabs = []
                    for im in range(2):
                        gsrc, usrc = slab_src[im]
                        sgl, isg = ws.load(gsrc, HC * 128)
                        sul, isu = ws.load(usrc, HC * 128)
                        for tq in range(HALF // 512):
                            bg, bu = (0, 1) if tq == 0 else (2, 3)
                            ring = tq
                            wg = [ws.ready(isg), lnA_ready,
                                  ("SC", bank_sc.get(bg, 0)),
                                  ("VE", bank_ve.get(bg, 0))] + half_entry
                            for hx in range(HC):
                                op("tensor", lambda e, s=sgl, hx=hx, tq=tq, bg=bg: e.matmul(
                                    bk(bg), ws.wsb[s][:, hx * 128:(hx + 1) * 128],
                                    lnA[:, hx * HALF + tq * 512:hx * HALF + (tq + 1) * 512],
                                    start=(hx == 0), stop=(hx == HC - 1)),
                                    waits=(wg if hx == 0 else []),
                                    inc=("PE" if hx == HC - 1 else None))
                            pe_g = now("PE")
                            wu = [ws.ready(isu), ("VE", bank_ve.get(bu, 0)),
                                  ("SC", bank_sc.get(bu, 0))]
                            for hx in range(HC):
                                op("tensor", lambda e, s=sul, hx=hx, tq=tq, bu=bu: e.matmul(
                                    bk(bu), ws.wsb[s][:, hx * 128:(hx + 1) * 128],
                                    lnA[:, hx * HALF + tq * 512:hx * HALF + (tq + 1) * 512],
                                    start=(hx == 0), stop=(hx == HC - 1)),
                                    waits=(wu if hx == 0 else []),
                                    inc=("PE" if hx == HC - 1 else None))
                            pe_u = now("PE")
                            lnA_pe_last = pe_u
                            sc_sg = op("scalar", lambda e, bg=bg, sg_=sgb[ring]: e.activation(
                                sg_[:], bk(bg), AF.Silu),
                                waits=[("PE", pe_g), ("VE", sgb_ve.get(ring, 0))]
                                + half_entry, inc="SC")
                            bank_sc[bg] = sc_sg
                            wuc = [("PE", pe_u), ("VE", usb_ve.get(ring, 0))] + half_entry
                            if shared:
                                ve_uc = op("vector", lambda e, bu=bu, ub_=usb[ring]: e.tensor_copy(
                                    ub_[:], bk(bu)), waits=wuc, inc="VE",
                                    pub=False)
                            else:
                                wuc.append(cb_ready)
                                ve_uc = op("vector",
                                           lambda e, bu=bu, tq=tq, ub_=usb[ring],
                                           cb_=cbs[im]:
                                           e.tensor_tensor(
                                               ub_[:], bk(bu),
                                               cb_[:, tq * 512:(tq + 1) * 512],
                                               ALU.mult), waits=wuc, inc="VE",
                                           pub=False)
                            bank_ve[bu] = ve_uc
                            dsta = acts[im][:, jc * HALF + tq * 512:
                                            jc * HALF + (tq + 1) * 512]
                            wact = [("SC", sc_sg)] + half_entry
                            if jc == 0 and tq == 0:
                                wact.append(("PE", prev_act_pe))
                            ve_a = op("vector", lambda e, d=dsta, sg_=sgb[ring],
                                      ub_=usb[ring]: e.tensor_mul(
                                d, sg_[:], ub_[:]), waits=wact, inc="VE")
                            sgb_ve[ring] = ve_a
                            usb_ve[ring] = ve_a
                            ve_act_last = ve_a
                        ws.done(isg, now("PE"))
                        ws.done(isu, now("PE"))
                if not shared:
                    cbs_ve_free = [ve_act_last, ve_act_last]
                nyj = J2
                for hx in range(HC):
                    if shared:
                        sy_a, isy_a = ws.load(
                            sw2[hx, :, 0:JS2 // 2, :].rearrange("p j k -> p (j k)"),
                            (JS2 // 2) * 128)
                        sy_b, isy_b = ws.load(
                            sw2[hx, :, JS2 // 2:JS2, :].rearrange("p j k -> p (j k)"),
                            (JS2 // 2) * 128)
                    else:
                        sy_a, isy_a = ws.load(w2[ea, hx].rearrange("p j k -> p (j k)"),
                                              J2 * 128)
                        sy_b, isy_b = ws.load(w2[eb, hx].rearrange("p j k -> p (j k)"),
                                              J2 * 128)
                    for tq in range(HALF // 512):
                        by = 4 + 2 * tq + (hx & 1)
                        wy = [ws.ready(isy_a), ws.ready(isy_b), ("VE", ve_act_last),
                              ("VE", bank_ve.get(by, 0)), ("SC", bank_sc.get(by, 0))]
                        first = True
                        for im, syl in ((0, sy_a), (1, sy_b)):
                            for j2x in range(nyj):
                                lastmm = (im == 1 and j2x == nyj - 1)
                                op("tensor",
                                   lambda e, s=syl, j2x=j2x, im=im, tq=tq, by=by,
                                   lm=lastmm, a_=acts[im]: e.matmul(
                                       bk(by), ws.wsb[s][:, j2x * 128:(j2x + 1) * 128],
                                       a_[:, j2x * HALF + tq * 512:
                                          j2x * HALF + (tq + 1) * 512],
                                       start=(im == 0 and j2x == 0), stop=lm),
                                   waits=(wy if first else []),
                                   inc=("PE" if lastmm else None))
                                first = False
                        pe_y = now("PE")
                        dsta = acc[:, hx * HALF + tq * 512:hx * HALF + (tq + 1) * 512]
                        if ip == 0:
                            v = op("scalar", lambda e, d=dsta, by=by: e.copy(d, bk(by)),
                                   waits=[("PE", pe_y), ("VE", acc_read_ve)], inc="SC")
                            bank_sc[by] = v
                        else:
                            v = op("vector", lambda e, d=dsta, by=by: e.tensor_tensor(
                                d, bk(by), d, ALU.add), waits=[("PE", pe_y)], inc="VE",
                                pub=False)
                            bank_ve[by] = v
                    ws.done(isy_a, now("PE"))
                    ws.done(isy_b, now("PE"))
                prev_act_pe = now("PE")
            ve_acc_done = E.fence("vector")
            sc_acc_done = now("SC")
            if half == 1 and "dbg_acc1" in dbg_t:
                for hcx in range(HC):
                    op("gpsimd", lambda e, hx=hcx: e.dma_start(
                        dbg_t["dbg_acc1"][hx * 128:(hx + 1) * 128, :],
                        acc[:, hx * HALF:(hx + 1) * HALF]),
                        waits=[("VE", ve_acc_done), ("SC", sc_acc_done)],
                        inc="sDBG", incv=16)
            if half == 1 and "dbg_lnr" in dbg_t:
                op("gpsimd", lambda e: e.dma_start(
                    dbg_t["dbg_lnr"][:, 0:HC * HALF], lnA[:]),
                    waits=[("VE", ve_acc_done)], inc="sDBG", incv=16)
            if half == 0 and "dbg_acc" in dbg_t:
                for hcx in range(HC):
                    op("gpsimd", lambda e, hx=hcx: e.dma_start(
                        dbg_t["dbg_acc"][hx * 128:(hx + 1) * 128, :],
                        acc[:, hx * HALF:(hx + 1) * HALF]),
                        waits=[("VE", ve_acc_done), ("SC", sc_acc_done)],
                        inc="sDBG", incv=16)

            p3h.close()
            p4h = contextlib.ExitStack()
            xpc = p4h.enter_context(nc.sbuf_tensor(f"xpc_{half}", [128, HC * 128], F32))
            p4_entry = [(s, E.cnt[s]) for s in list(E.cnt)]
            for t4 in range(HALF // 128):
                wdx = [("sX1", dx_x1_done)] + (p4_entry if t4 == 0 else [])
                if dx_out:
                    wdx += [("VE", ve_out_done), ("sOUT", dx_out)]
                dvx = op("gpsimd", lambda e, t4=t4, t0=t0, xp_=xpc: e.dma_start(
                    xp_[:].rearrange("p (h t) -> p h t", h=HC),
                    x1T[:, t0 + t4 * 128:t0 + (t4 + 1) * 128]
                    .rearrange("(h p) t -> p h t", p=128)),
                    waits=wdx, inc="sXPC", incv=16)
                d4 = emit_ln_stats(
                    lambda hx, t4=t4: acc[:, hx * HALF + t4 * 128:
                                          hx * HALF + t4 * 128 + 128],
                    128, 2, 3, 4, 5,
                    [("VE", ve_acc_done), ("SC", sc_acc_done)])
                last_ve = 0
                for hx in range(HC):
                    r = hx % 2
                    w = [("PE", d4["pe_bcast"])] + ([("sXPC", dvx)] if hx == 0 else [])
                    op("vector", lambda e, hx=hx, r=r, t4=t4: e.tensor_sub(
                        fsc[:, r * 512:r * 512 + 128],
                        acc[:, hx * HALF + t4 * 128:hx * HALF + t4 * 128 + 128],
                        bk(4, 0, 128)), waits=w, inc="VE", pub=False)
                    op("vector", lambda e, r=r: e.tensor_mul(
                        fsc[:, r * 512:r * 512 + 128], fsc[:, r * 512:r * 512 + 128],
                        bk(5, 0, 128)), inc="VE", pub=False)
                    last_ve = op("vector", lambda e, hx=hx, r=r, xp_=xpc: e.tensor_add(
                        xp_[:, hx * 128:(hx + 1) * 128], xp_[:, hx * 128:(hx + 1) * 128],
                        fsc[:, r * 512:r * 512 + 128]), inc="VE",
                        pub=(hx == HC - 1))
                ve_out_done = last_ve
                acc_read_ve = now("VE")
                dx_out = op("gpsimd", lambda e, t4=t4, t0=t0, xp_=xpc: e.dma_start(
                    outT[:, t0 + t4 * 128:t0 + (t4 + 1) * 128]
                    .rearrange("(h p) t -> p h t", p=128),
                    xp_[:].rearrange("p (h t) -> p h t", h=HC)),
                    waits=[("VE", last_ve)], inc="sOUT", incv=16)
            for bnk in range(6):
                bank_ve[bnk] = max(bank_ve.get(bnk, 0), now("VE"))
            p4h.close()
        ws.mode = None
        fin = [("sOUT", E.cnt.get("sOUT", 0))]
        if dbg_t:
            fin.append(("sDBG", E.cnt.get("sDBG", 0)))
        op("gpsimd", lambda e: e.engine_nop(), waits=fin, inc=None)
        p3.close()

        block = en(nc.Block())
        E.flush(block)
    return nc


def prep_shared(c, c_attn_w, c_attn_b, c_proj_w, c_proj_b, gate_w, w13, w2, sw13, sw2):
    HC, NE, J2, JS2 = c.HC, c.NE, c.J2, c.JS2
    f = np.ascontiguousarray
    aw = f(c_attn_w.reshape(HC, 128, 3 * HC, 128).transpose(2, 1, 0, 3))
    ab = f(c_attn_b.reshape(3 * HC, 128).T)
    pw = f(c_proj_w.reshape(HC, 128, HC, 128).transpose(2, 1, 0, 3))
    pb = f(c_proj_b.reshape(HC, 128).T)
    gw = f(gate_w.T.reshape(HC, 128, NE))
    w13b = f(w13.reshape(NE, 2 * J2, 128, HC, 128).transpose(0, 1, 4, 3, 2))
    w2b = f(w2.reshape(NE, HC, 128, J2, 128).transpose(0, 1, 4, 3, 2))
    sw13b = f(sw13.reshape(2 * JS2, 128, HC, 128).transpose(0, 3, 2, 1))
    sw2b = f(sw2.reshape(HC, 128, JS2, 128).transpose(0, 3, 2, 1))
    ident = np.eye(128, dtype=np.float32)
    ii, jj = np.meshgrid(np.arange(128), np.arange(128), indexing="ij")
    cmask = np.where(jj <= ii, 0.0, NEGM).astype(np.float32)
    cmaskT = np.ascontiguousarray(cmask.T)
    cstv = f(np.concatenate([ident, cmask, cmaskT], axis=1))
    return {"aw": aw, "ab": ab, "pw": pw, "pb": pb, "gw": gw, "w13": w13b,
            "w2": w2b, "sw13": sw13b, "sw2": sw2b, "cst": cstv}


def kernel(L0_attn_hidden_states, L0_attn_c_attn_w, L0_attn_c_attn_b,
           L0_attn_c_proj_w, L0_attn_c_proj_b, L0_ffn_gate_w, L0_ffn_w13,
           L0_ffn_w2, L0_ffn_sw13, L0_ffn_sw2):
    c = CFG()
    x = np.asarray(L0_attn_hidden_states, dtype=np.float32)
    B = x.shape[0]
    shared = prep_shared(
        c, np.asarray(L0_attn_c_attn_w, np.float32),
        np.asarray(L0_attn_c_attn_b, np.float32),
        np.asarray(L0_attn_c_proj_w, np.float32),
        np.asarray(L0_attn_c_proj_b, np.float32),
        np.asarray(L0_ffn_gate_w, np.float32), np.asarray(L0_ffn_w13, np.float32),
        np.asarray(L0_ffn_w2, np.float32), np.asarray(L0_ffn_sw13, np.float32),
        np.asarray(L0_ffn_sw2, np.float32))
    nc = build_nc(c)
    in_maps = []
    for b in range(B):
        m = dict(shared)
        m["xT"] = np.ascontiguousarray(x[b].T)
        in_maps.append(m)
    res = run_bass_kernel_spmd(nc, in_maps, core_ids=list(range(B)))
    out = np.stack([np.asarray(r["outT"]).T for r in res.results], axis=0)
    return out.astype(np.float32)
